# version 07
# baseline (speedup 1.0000x reference)
import contextlib
import numpy as np
import ml_dtypes
import concourse.bass as bass
import concourse.mybir as mybir
from concourse.bass_utils import run_bass_kernel_spmd

F32 = mybir.dt.float32
BF16 = mybir.dt.bfloat16
I32 = mybir.dt.int32
AF = mybir.ActivationFunctionType
ALU = mybir.AluOpType
AX = mybir.AxisListType
BF = ml_dtypes.bfloat16

SEMS = ["cT", "cS", "cV", "cG",
        "iniP", "lx0", "lx1", "xs0", "xs1", "xlx", "qw0", "qw1",
        "lq0", "lq1", "lq2", "lq3",
        "ws0", "ws1", "ws2", "ws3", "ws4", "ws5", "ws6", "ws7",
        "l2qb", "w1xb", "aoq2",
        "vw", "qsp", "mskq", "kq0", "kq1",
        "aoq", "x2q", "wo0", "wo1", "l2q", "w1x", "w2x", "yq"]

D = 2048
T = 2048
TO = 1024
H = 16
NT = 16
KJ = [16, 12, 8, 4]
NPAIR = 4
PAIRS_A = [(14, 15), (8, 9), (6, 7), (0, 1)]
PAIRS_B = [(12, 13), (10, 11), (4, 5), (2, 3)]
QSCALE = 1.0 / float(np.sqrt(128.0))
INV_D = 1.0 / D
EPS = 1e-5
NGRP = 8
GF = 1024
NQU = 128
ORD = []
for _g in range(NGRP):
    ORD += [16 * _g + _k for _k in range(16)]
    ORD += [NQU + _r * 8 + _g for _r in range(16)]
assert sorted(ORD) == list(range(2 * NQU))
_f = np.float32
_CHD = 0.7 / 16777216.0
CH = float(_f(_CHD))
CL = float(_f(_CHD - CH))
_c = _f(_f(CH) * _f(4097.0))
CH_A = float(_f(_c - _f(_c - _f(CH))))
CH_B = float(_f(_f(CH) - _f(CH_A)))
RND_C = 12582912.0
SPL_C = 12582912.0
ULP_PUSH = float(_f(2.0 ** -24 * 1.001))


KLIST = {
    16: list(range(16)),
    12: [2, 3, 4, 5, 6, 7, 10, 11, 12, 13, 14, 15],
    8: [4, 5, 6, 7, 12, 13, 14, 15],
    4: [6, 7, 14, 15],
}
MLOC = {j: [2 * j, 2 * j + 1, 8 + 2 * j, 9 + 2 * j] for j in range(4)}


def _chunks(n, c=6):
    out, i = [], 0
    while i < n:
        out.append((i, min(c, n - i)))
        i += c
    return out


class _DryInst:
    def then_inc(self, sem, k):
        return self


class _DryReg:
    def __enter__(self):
        return self

    def __exit__(self, *a):
        return False


class _DryEng:
    def register(self, *a, **k):
        return _DryReg()

    def snap(self, *a, **k):
        return None

    def wait_ge(self, *a, **k):
        return None

    def __getattr__(self, name):
        def f(*a, **k):
            return _DryInst()
        return f


class Ctx:
    def __init__(self):
        self.dry = True
        self.ctr = {n: 0 for n in SEMS}
        self.ev = {}
        self.sem = {}

    def reset(self):
        self.dry = False
        for k in self.ctr:
            self.ctr[k] = 0

    def bump(self, cname, inst, k=1):
        if not self.dry:
            inst.then_inc(self.sem[cname], k)
        self.ctr[cname] += k
        return self.ctr[cname]

    def dma(self, cname, inst):
        return self.bump(cname, inst, 16)

    def mark(self, name, cname):
        if self.dry:
            assert name not in self.ev, f"duplicate event {name}"
            self.ev[name] = (cname, self.ctr[cname])
        else:
            assert self.ev[name] == (cname, self.ctr[cname]), \
                f"pass mismatch at {name}: dry {self.ev[name]} real {(cname, self.ctr[cname])}"

    def wait(self, eng, name):
        if self.dry:
            return
        cname, v = self.ev[name]
        if v > 0:
            eng.wait_ge(self.sem[cname], v)

    def chain(self, eng, cname, inst):
        n = self.bump(cname, inst)
        if not self.dry:
            eng.wait_ge(self.sem[cname], n)
        return n


def build_program(debug=()):
    nc = bass.Bass("TRN2", target_bir_lowering=False)

    def dk(name):
        return "ExternalOutput" if name in debug else "Internal"

    xg = nc.dram_tensor("xg", [T, D], F32, kind="ExternalInput").ap()
    masks = nc.dram_tensor("masks", [128, 4096], BF16, kind="ExternalInput").ap()
    wqkv = nc.dram_tensor("wqkv", [D, 3 * D], F32, kind="ExternalInput").ap()
    bqkv = nc.dram_tensor("bqkv", [3 * D], F32, kind="ExternalInput").ap()
    wout = nc.dram_tensor("wout", [D, D], F32, kind="ExternalInput").ap()
    bout = nc.dram_tensor("bout", [D], F32, kind="ExternalInput").ap()
    w1 = nc.dram_tensor("w1", [8192, D], F32, kind="ExternalInput").ap()
    b1 = nc.dram_tensor("b1", [8192], F32, kind="ExternalInput").ap()
    w2 = nc.dram_tensor("w2", [D, 8192], F32, kind="ExternalInput").ap()
    b2 = nc.dram_tensor("b2", [D], F32, kind="ExternalInput").ap()
    y = nc.dram_tensor("y", [TO, D], F32, kind="ExternalOutput").ap()

    xln_hbm = nc.dram_tensor("xln_hbm", [T, D], BF16, kind=dk("xln_hbm")).ap()
    q_hbm = nc.dram_tensor("q_hbm", [D, TO], BF16, kind=dk("q_hbm")).ap()
    k_hbm = nc.dram_tensor("k_hbm", [D, T], BF16, kind=dk("k_hbm")).ap()
    ao_hbm = nc.dram_tensor("ao_hbm", [TO, D], BF16, kind=dk("ao_hbm")).ap()
    x2ln_hbm = nc.dram_tensor("x2ln_hbm", [TO, D], BF16, kind=dk("x2ln_hbm")).ap()
    w1q_hbm = nc.dram_tensor("w1q_hbm", [8192, D], BF16, kind=dk("w1q_hbm")).ap()
    w2q_hbm = nc.dram_tensor("w2q_hbm", [D, 8192], BF16, kind=dk("w2q_hbm")).ap()
    dbg_pt = None
    if "dbg_pt" in debug:
        dbg_pt = nc.dram_tensor("dbg_pt", [128, 4608], BF16, kind="ExternalOutput").ap()

    wqkv_r = wqkv.rearrange("(dc p) n -> p dc n", p=128)
    wout_r = wout.rearrange("(cc p) n -> p cc n", p=128)

    with contextlib.ExitStack() as stk:
        def sbt(name, shape, dt):
            return stk.enter_context(nc.sbuf_tensor(name, shape, dt))

        A32t = sbt("A32", [128, 16384], F32)
        A32 = A32t.ap()
        A16 = A32.bitcast(BF16)
        B16 = sbt("B16", [128, 33024], BF16).ap()
        C32 = sbt("C32", [128, 12288], F32).ap()
        C16 = C32.bitcast(BF16)
        p1buf = sbt("p1buf", [128, 2048], F32).ap()
        p1buf2 = sbt("p1buf2", [128, 1024], F32).ap()
        evq = sbt("evq", [128, 2048], BF16).ap()
        cst = sbt("cst", [128, 8], F32).ap()
        onesb = sbt("onesb", [128, 128], BF16).ap()
        ones32r = sbt("ones32r", [1, 128], F32).ap()
        bqcols = sbt("bqcols", [128, 32], F32).ap()
        qbias = sbt("qbias", [128, 16], F32).ap()
        b1cols = sbt("b1cols", [128, 64], F32).ap()
        brows = sbt("brows", [1, 6144], BF16).ap()
        statc = sbt("statc", [128, 64], F32).ap()
        srow = sbt("srow", [1, 64], F32).ap()
        pacc = sbt("pacc", [128, 256], F32).ap()

        psA = stk.enter_context(nc.psum_tensor("psA", [128, 2048], F32)).ap()
        psB = stk.enter_context(nc.psum_tensor("psB", [128, 2048], F32)).ap()

        X = Ctx()
        for cn in SEMS:
            X.sem[cn] = stk.enter_context(nc.semaphore(cn))

        xlnv = lambda dc: A16[:, dc * 2048:(dc + 1) * 2048]
        ao_tm = lambda tt: A16[:, tt * 2048:(tt + 1) * 2048]
        x2v = lambda tt: A32[:, tt * 2048:(tt + 1) * 2048]
        vslot = lambda h, s: B16[:, (h * NT + s) * 129:(h * NT + s) * 129 + 129]
        ao_fm = lambda cc: B16[:, cc * 1024:(cc + 1) * 1024]
        w1qTv = lambda dc: B16[:, 16384 + dc * 1024:16384 + (dc + 1) * 1024]
        w2qTv = lambda fc: B16[:, fc * 2048:(fc + 1) * 2048]
        xt = [C32[:, 0:2048], C32[:, 2048:4096]]
        sqf = [C32[:, 4096:6144], C32[:, 6144:8192]]
        lntm = [C16[:, 16384:18432], C16[:, 18432:20480]]
        qk32 = [C32[:, 0:4096], C32[:, 4096:8192]]
        qkbf = [C16[:, 16384:20480], C16[:, 20480:24576]]
        v32 = C32[:, 0:4096]
        vbf = [C16[:, 16384:20480], C16[:, 20480:24576]]
        kTb = [C16[:, 0:2048], C16[:, 2048:4096]]
        qTb = [C16[:, 4096:5120], C16[:, 5120:6144]]
        woutst = [C16[:, 0:8192], C16[:, 8192:16384]]
        x2lnfm = lambda dc: C16[:, dc * 1024:(dc + 1) * 1024]
        hgv = lambda fc: C16[:, 16384 + fc * 1024:16384 + (fc + 1) * 1024]
        ln2sq = C32[:, 4096:6144]
        p1s = [p1buf[:, 0:1024], p1buf[:, 1024:2048], p1buf2[:, 0:1024]]
        p1tv4 = [evq[:, 0:1024], evq[:, 1024:2048],
                 C16[:, 9216:10240], C16[:, 10240:11264]]
        NQIL = 192

        def ptb(u):
            return u % 4 if u < NQIL else u % 2

        def ptprev(u):
            return u - 4 if u < NQIL + 2 else u - 2
        PTv = [C16[:, 6144:7680], C16[:, 7680:9216]]
        msk = C16[:, 11264:15360]
        x2st = C16[:, 14336:16384]

        LN_S0, LN_S2, LN_MEAN, LN_E2, LN_VAR, LN_RSTD, LN_NMR = 0, 1, 2, 3, 4, 5, 6
        NEGC = 8
        T1C, NT1C, T2C, NT2C = 26, 27, 28, 29
        RECC = 30

        sched = [_chunks(KJ[j]) for j in range(NPAIR)]
        nchunk_pair = [len(s) for s in sched]
        gchunk_of = {}
        g = 0
        for h in range(H):
            for j in range(NPAIR):
                for ci in range(nchunk_pair[j]):
                    gchunk_of[(h, j, ci)] = g
                    g += 1
        NGCH = g

        def p0_sync(q):
            pass

        def p0_pool(gp):
            X.dma("iniP", gp.dma_start(out=bqcols, in_=bqkv[0:4096].rearrange("(m p) -> p m", p=128),
                                     allow_slow_non_contiguous=True))
            X.dma("iniP", gp.dma_start(out=b1cols, in_=b1.rearrange("(m p) -> p m", p=128),
                                     allow_slow_non_contiguous=True))
            X.dma("iniP", gp.dma_start(out=brows[0:1, 0:2048],
                                     in_=bqkv.rearrange("(a n) -> a n", a=3)[2:3, :]))
            X.dma("iniP", gp.dma_start(out=brows[0:1, 2048:4096],
                                     in_=bout.rearrange("(a n) -> a n", a=1)))
            X.dma("iniP", gp.dma_start(out=brows[0:1, 4096:6144],
                                     in_=b2.rearrange("(a n) -> a n", a=1)))
            X.mark("init_pool", "iniP")

        def p0_vec(v):
            v.memset(cst[:, 0:1], 1.0)
            v.memset(cst[:, 1:2], EPS)
            v.memset(cst[:, 2:3], QSCALE)
            v.memset(onesb, 1.0)
            v.memset(ones32r, 1.0)
            X.bump("cV", v.memset(B16[:, 128:H * NT * 129:129], 1.0))
            X.wait(v, "init_pool")
            X.bump("cV", v.tensor_scalar(out=qbias, in0=bqcols[:, 0:16], scalar1=QSCALE,
                                         scalar2=None, op0=ALU.mult))
            X.mark("init_vec", "cV")

        def p1_sync(q):
            for i in range(NT):
                if i >= 2:
                    X.wait(q, f"ln1_norm_{i-2}")
                    X.wait(q, f"ln1_r0_{i-2}")
                X.dma(f"lx{i % 2}", q.dma_start(out=xt[i % 2], in_=xg[i * 128:(i + 1) * 128, :]))
                X.mark(f"ln1_ld_{i}", f"lx{i % 2}")

        def p1_scalar(s):
            for i in range(NT):
                X.wait(s, f"ln1_ld_{i}")
                if i >= 2:
                    X.wait(s, f"ln1_sp_{i-2}")
                    X.wait(s, f"ln1_r2_{i-2}")
                X.bump("cS", s.square(out=sqf[i % 2], in_=xt[i % 2]))
                X.mark(f"ln1_sq_{i}", "cS")
                X.wait(s, f"ln1_var_{i}")
                X.chain(s, "cS", s.activation(out=statc[:, LN_RSTD:LN_RSTD + 1],
                                              in_=statc[:, LN_VAR:LN_VAR + 1], func=AF.Ln,
                                              bias=cst[:, 1:2]))
                X.chain(s, "cS", s.activation(out=statc[:, LN_RSTD:LN_RSTD + 1],
                                              in_=statc[:, LN_RSTD:LN_RSTD + 1], func=AF.Exp,
                                              scale=-0.5))
                X.mark(f"ln1_rstd_{i}", "cS")
                X.wait(s, f"ln1_nmr_{i}")
                X.chain(s, "cS", s.activation(out=lntm[i % 2], in_=xt[i % 2], func=AF.Identity,
                                              scale=statc[:, LN_RSTD:LN_RSTD + 1],
                                              bias=statc[:, LN_NMR:LN_NMR + 1]))
                X.mark(f"ln1_norm_{i}", "cS")
                X.dma(f"xs{i % 2}", s.dma_start(out=xln_hbm[i * 128:(i + 1) * 128, :], in_=lntm[i % 2]))
                X.mark(f"ln1_sp_{i}", f"xs{i % 2}")
            X.wait(s, "ln1_sp_14")
            X.wait(s, "ln1_sp_15")
            for dc in range(NT):
                X.dma("xlx", s.dma_start(out=xlnv(dc), in_=xln_hbm[:, dc * 128:(dc + 1) * 128],
                                        transpose=True))
            X.mark("xln_fm", "xlx")

        def p1_vec(v):
            for i in range(NT):
                X.wait(v, f"ln1_ld_{i}")
                if i >= 1:
                    X.wait(v, f"ln1_norm_{i-1}")
                X.chain(v, "cV", v.tensor_reduce(out=statc[:, LN_S0:LN_S0 + 1], in_=xt[i % 2],
                                                 axis=AX.X, op=ALU.add))
                X.mark(f"ln1_r0_{i}", "cV")
                X.wait(v, f"ln1_sq_{i}")
                X.chain(v, "cV", v.tensor_reduce(out=statc[:, LN_S2:LN_S2 + 1], in_=sqf[i % 2],
                                                 axis=AX.X, op=ALU.add))
                X.mark(f"ln1_r2_{i}", "cV")
                X.chain(v, "cV", v.tensor_scalar(out=statc[:, LN_MEAN:LN_MEAN + 1],
                                                 in0=statc[:, LN_S0:LN_S0 + 1], scalar1=INV_D,
                                                 scalar2=None, op0=ALU.mult))
                X.chain(v, "cV", v.tensor_scalar(out=statc[:, LN_E2:LN_E2 + 1],
                                                 in0=statc[:, LN_S2:LN_S2 + 1], scalar1=INV_D,
                                                 scalar2=None, op0=ALU.mult))
                X.chain(v, "cV", v.tensor_mul(out=statc[:, LN_VAR:LN_VAR + 1],
                                              in0=statc[:, LN_MEAN:LN_MEAN + 1],
                                              in1=statc[:, LN_MEAN:LN_MEAN + 1]))
                X.chain(v, "cV", v.tensor_sub(out=statc[:, LN_VAR:LN_VAR + 1],
                                              in0=statc[:, LN_E2:LN_E2 + 1],
                                              in1=statc[:, LN_VAR:LN_VAR + 1]))
                X.mark(f"ln1_var_{i}", "cV")
                X.wait(v, f"ln1_rstd_{i}")
                X.chain(v, "cV", v.tensor_scalar(out=statc[:, LN_NMR:LN_NMR + 1],
                                                 in0=statc[:, LN_MEAN:LN_MEAN + 1],
                                                 scalar1=statc[:, LN_RSTD:LN_RSTD + 1],
                                                 scalar2=-1.0, op0=ALU.mult, op1=ALU.mult))
                X.mark(f"ln1_nmr_{i}", "cV")
            X.mark("ln1_vdone", "cV")

        def p1src(u):
            if u < NQU:
                r, cp = u // 2, u % 2
                return w1[r * 128:(r + 1) * 128, cp * 1024:(cp + 1) * 1024]
            r, cp = (u - NQU) // 8, (u - NQU) % 8
            return w2[r * 128:(r + 1) * 128, cp * 1024:(cp + 1) * 1024]

        def pqdst(u):
            if u < NQU:
                r, cp = u // 2, u % 2
                return w1q_hbm[r * 128:(r + 1) * 128, cp * 1024:(cp + 1) * 1024]
            r, cp = (u - NQU) // 8, (u - NQU) % 8
            return w2q_hbm[r * 128:(r + 1) * 128, cp * 1024:(cp + 1) * 1024]

        def p2_sync(q):
            X.wait(q, "ln1_norm_15")
            X.wait(q, "ln1_vdone")
            for r in range(16):
                if r >= 2:
                    X.wait(q, f"qk_conv_{r-2}")
                c0 = r * 256 if r < 8 else 2048 + (r - 8) * 256
                for d4 in range(4):
                    X.dma(f"qw{r % 2}", q.dma_start(
                        out=qk32[r % 2][:, d4 * 1024:(d4 + 1) * 1024]
                            .rearrange("p (dc n) -> p dc n", n=256),
                        in_=wqkv_r[:, d4 * 4:(d4 + 1) * 4, c0:c0 + 256]))
                X.mark(f"qk_ld_{r}", f"qw{r % 2}")
                for k in range(16):
                    u = r * 16 + k
                    if u >= 3:
                        X.wait(q, f"p1red_{u-3}")
                    src = p1src(ORD[u])
                    for qr in range(8):
                        X.dma(f"lq{u % 3}", q.dma_start(
                            out=p1s[u % 3][qr * 16:(qr + 1) * 16, :],
                            in_=src[qr * 16:(qr + 1) * 16, :]))
                    X.mark(f"p1ld_{u}", f"lq{u % 3}")
            for vb in range(8):
                if vb == 0:
                    X.wait(q, "qk_conv_14")
                if vb >= 1:
                    X.wait(q, f"v_conv_{vb-1}")
                c0 = 4096 + vb * 256
                for d4 in range(4):
                    X.dma("vw", q.dma_start(
                        out=v32[:, d4 * 1024:(d4 + 1) * 1024]
                            .rearrange("p (dc n) -> p dc n", n=256),
                        in_=wqkv_r[:, d4 * 4:(d4 + 1) * 4, c0:c0 + 256]))
                X.mark(f"v_ld_{vb}", "vw")

        def p2_tensor(t):
            X.wait(t, "xln_fm")
            if True:
                for m in range(32):
                    X.wait(t, f"qk_conv_{m // 2}")
                    if m >= 2:
                        X.wait(t, f"qk_evd_{m-2}")
                    ps = psA if m % 2 == 0 else psB
                    inst = None
                    ntt = 2 if m < 16 else 4
                    for dc in range(16):
                        lhs = qkbf[(m // 2) % 2][:, dc * 256 + (m % 2) * 128:
                                                 dc * 256 + (m % 2) * 128 + 128]
                        for t4 in range(ntt):
                            inst = t.matmul(ps[:, t4 * 512:(t4 + 1) * 512], lhs,
                                            xlnv(dc)[:, t4 * 512:(t4 + 1) * 512],
                                            start=(dc == 0), stop=(dc == 15))
                    X.bump("cT", inst)
                    X.mark(f"qk_mm_{m}", "cT")
            X.wait(t, "init_pool")
            for vb in range(8):
                X.wait(t, f"v_conv_{vb}")
                for tt in range(16):
                    u = vb * 16 + tt
                    bank = tt % 8
                    ps = psA if bank < 4 else psB
                    po = ps[:, (bank % 4) * 512:(bank % 4) * 512 + 256]
                    if u == 0:
                        X.wait(t, "qk_evd_30")
                    if u == 4:
                        X.wait(t, "qk_evd_31")
                    if u >= 8:
                        X.wait(t, f"v_ev_{u-8}")
                    for dc in range(16):
                        t.matmul(po, xlnv(dc)[:, tt * 128:(tt + 1) * 128],
                                 vbf[vb % 2][:, dc * 256:(dc + 1) * 256],
                                 start=(dc == 0), stop=False)
                    inst = t.matmul(po, onesb[0:1, :], brows[0:1, vb * 256:vb * 256 + 256],
                                    start=False, stop=True)
                    X.bump("cT", inst)
                    X.mark(f"v_mm_{u}", "cT")
            X.mark("qkv_done", "cT")

        def p2_vec(v):
            for r in range(16):
                X.wait(v, f"qk_ld_{r}")
                if r == 0:
                    X.wait(v, "ln1_sp_14")
                    X.wait(v, "ln1_sp_15")
                if r >= 2:
                    X.wait(v, f"qk_mm_{2 * r - 3}")
                X.bump("cV", v.tensor_copy(out=qkbf[r % 2], in_=qk32[r % 2]))
                X.mark(f"qk_conv_{r}", "cV")
        def conv_vb(v, vb):
            X.wait(v, f"v_ld_{vb}")
            if vb == 0:
                X.wait(v, "qk_mm_31")
            if vb >= 2:
                X.wait(v, f"v_mm_{(vb-2)*16+15}")
            X.bump("cV", v.tensor_copy(out=vbf[vb % 2], in_=v32))
            X.mark(f"v_conv_{vb}", "cV")

        def p2_vec_b(v):
            conv_vb(v, 0)
            for vb in range(8):
                if vb + 1 < 8:
                    conv_vb(v, vb + 1)
                for tt in range(16):
                    u = vb * 16 + tt
                    X.wait(v, f"v_mm_{u}")
                    bank = tt % 8
                    ps = psA if bank < 4 else psB
                    po = ps[:, (bank % 4) * 512:(bank % 4) * 512 + 256]
                    X.bump("cV", v.tensor_copy(out=vslot(2 * vb, tt)[:, 0:128], in_=po[:, 0:128]))
                    X.bump("cV", v.tensor_copy(out=vslot(2 * vb + 1, tt)[:, 0:128],
                                               in_=po[:, 128:256]))
                    X.mark(f"v_ev_{u}", "cV")
            X.mark("v_all", "cV")

        def p2_scalar(s):
            stp = {"u": 0}

            def p1red(k):
                for _ in range(k):
                    u = stp["u"]
                    if u >= 2 * NQU:
                        return
                    X.wait(s, f"p1ld_{u}")
                    lu = ORD[u]
                    X.bump("cS", s.activation(out=p1s[u % 3], in_=p1s[u % 3], func=AF.Abs,
                                              accum_out=pacc[:, lu:lu + 1]))
                    X.mark(f"p1red_{u}", "cS")
                    stp["u"] += 1
            for m in range(32):
                p1red(8)
                X.wait(s, f"qk_mm_{m}")
                if m >= 1:
                    X.wait(s, f"qk_sp_{m-1}")
                ps = psA if m % 2 == 0 else psB
                if m < 16:
                    for t4 in range(2):
                        X.bump("cS", s.activation(out=evq[:, t4 * 512:(t4 + 1) * 512],
                                                  in_=ps[:, t4 * 512:(t4 + 1) * 512],
                                                  func=AF.Identity, scale=cst[:, 2:3],
                                                  bias=qbias[:, m:m + 1]))
                    X.mark(f"qk_evd_{m}", "cS")
                    X.wait(s, f"qk_evd_{m}")
                    X.dma("qsp", s.dma_start(out=q_hbm[m * 128:(m + 1) * 128, :],
                                            in_=evq[:, 0:1024]))
                else:
                    for t4 in range(4):
                        X.bump("cS", s.activation(out=evq[:, t4 * 512:(t4 + 1) * 512],
                                                  in_=ps[:, t4 * 512:(t4 + 1) * 512],
                                                  func=AF.Identity, scale=cst[:, 0:1],
                                                  bias=bqcols[:, m:m + 1]))
                    X.mark(f"qk_evd_{m}", "cS")
                    X.wait(s, f"qk_evd_{m}")
                    X.dma("qsp", s.dma_start(out=k_hbm[(m - 16) * 128:(m - 15) * 128, :],
                                            in_=evq[:, 0:2048]))
                X.mark(f"qk_sp_{m}", "qsp")
            p1red(2 * NQU)
            X.mark("p1_all", "cS")
            X.mark("qkv_spills", "qsp")


        def p2c_sync(q):
            X.wait(q, "qkv_spills")
            X.wait(q, "v_conv_7")
            for h in range(H):
                if h >= 2:
                    X.wait(q, f"bnred_{h-2}")
                X.dma(f"kq{h % 2}", q.dma_start(out=kTb[h % 2], in_=k_hbm[h * 128:(h + 1) * 128, :]))
                X.dma(f"kq{h % 2}", q.dma_start(out=qTb[h % 2], in_=q_hbm[h * 128:(h + 1) * 128, :]))
                X.mark(f"bn_ld_{h}", f"kq{h % 2}")

        def p2c_scalar(s):
            X.wait(s, "qkv_spills")
            for h in range(H):
                X.wait(s, f"bn_ld_{h}")
                if h >= 1:
                    X.wait(s, f"bnq_{h-1}")
                X.bump("cS", s.square(out=evq[:, 0:2048], in_=kTb[h % 2]))
                X.mark(f"bnsqk_{h}", "cS")
                X.wait(s, f"bnk_{h}")
                X.bump("cS", s.square(out=evq[:, 0:1024], in_=qTb[h % 2]))
                X.mark(f"bnsqq_{h}", "cS")
            X.wait(s, "bnred_all")
            X.chain(s, "cS", s.activation(out=srow[0:1, 32:64], in_=srow[0:1, 32:64], func=AF.Ln))
            X.mark("bnln", "cS")
            X.wait(s, "bnadd")
            X.chain(s, "cS", s.activation(out=srow[0:1, 32:48], in_=srow[0:1, 32:48], func=AF.Exp,
                                          scale=0.5))
            X.mark("bnexp", "cS")

        def p2c_tensor(t):
            X.wait(t, "v_all")
            for h in range(H):
                X.wait(t, f"bnsqk_{h}")
                if h >= 1:
                    X.wait(t, f"bnred_{h-1}")
                inst = None
                for p_ in range(4):
                    inst = t.matmul(psA[0:1, p_ * 512:(p_ + 1) * 512], onesb[:, 0:1],
                                    evq[:, p_ * 512:(p_ + 1) * 512], start=True, stop=True)
                X.bump("cT", inst)
                X.mark(f"bnk_{h}", "cT")
                X.wait(t, f"bnsqq_{h}")
                inst = t.matmul(psB[0:1, 0:512], onesb[:, 0:1], evq[:, 0:512],
                                start=True, stop=True)
                inst = t.matmul(psB[0:1, 512:1024], onesb[:, 0:1], evq[:, 512:1024],
                                start=True, stop=True)
                X.bump("cT", inst)
                X.mark(f"bnq_{h}", "cT")
            X.wait(t, "bnneg")
            X.bump("cT", t.matmul(psB[:, 0:16], ones32r, srow[0:1, 32:48],
                                  start=True, stop=True))
            X.mark("bnt", "cT")

        def p2c_vec(v):
            for h in range(H):
                X.wait(v, f"bnk_{h}")
                X.chain(v, "cV", v.reduce_max(out=srow[0:1, 32 + h:33 + h],
                                              in_=psA[0:1, 0:2048], axis=AX.X))
                X.wait(v, f"bnq_{h}")
                X.chain(v, "cV", v.reduce_max(out=srow[0:1, 48 + h:49 + h],
                                              in_=psB[0:1, 0:1024], axis=AX.X))
                X.mark(f"bnred_{h}", "cV")
            X.mark("bnred_all", "cV")
            X.wait(v, "bnln")
            X.chain(v, "cV", v.tensor_add(out=srow[0:1, 32:48], in0=srow[0:1, 32:48],
                                          in1=srow[0:1, 48:64]))
            X.mark("bnadd", "cV")
            X.wait(v, "bnexp")
            X.chain(v, "cV", v.tensor_scalar(out=srow[0:1, 32:48], in0=srow[0:1, 32:48],
                                             scalar1=-1.0, scalar2=-1.0,
                                             op0=ALU.mult, op1=ALU.add))
            X.mark("bnneg", "cV")
            X.wait(v, "bnt")
            X.chain(v, "cV", v.tensor_copy(out=statc[:, 8:24], in_=psB[:, 0:16]))
            X.mark("negc_all", "cV")

        QS = {"ld": 0, "sp": 0, "q": 0, "g": 0}

        def qld_issue(q, k):
            for _ in range(k):
                u = QS["ld"]
                if u >= 2 * NQU:
                    return
                if u < 3:
                    X.wait(q, "p1_all")
                else:
                    X.wait(q, f"wqf_{u-3}")
                src = p1src(ORD[u])
                for qr in range(8):
                    X.dma(f"lq{u % 3}", q.dma_start(
                        out=p1s[u % 3][qr * 16:(qr + 1) * 16, :],
                        in_=src[qr * 16:(qr + 1) * 16, :]))
                X.mark(f"wqld_{u}", f"lq{u % 3}")
                QS["ld"] += 1

        def qsp_issue(q, k):
            for _ in range(k):
                u = QS["sp"]
                if u >= 2 * NQU:
                    return
                X.wait(q, f"wqf_{u}")
                dst = pqdst(ORD[u])
                for hh in range(2):
                    X.dma(f"ws{2 * ptb(u) + hh}", q.dma_start(
                        out=dst[hh * 64:(hh + 1) * 64, :],
                        in_=p1tv4[ptb(u)][hh * 64:(hh + 1) * 64, :]))
                    X.mark(f"wqsp{hh}_{u}", f"ws{2 * ptb(u) + hh}")
                if (u + 1) % 32 == 0:
                    g_ = (u + 1) // 32 - 1
                    for sw in range(8):
                        X.mark(f"qgrp_{g_}_{sw}", f"ws{sw}")
                QS["sp"] += 1

        def quant(v, k):
            for _ in range(k):
                u = QS["q"]
                if u >= 2 * NQU:
                    return
                if u == 0:
                    X.wait(v, "qkv_spills")
                    X.wait(v, "bnq_15")
                X.wait(v, f"wqld_{u}")
                pu = ptprev(u)
                if pu >= 0:
                    for hh in range(2):
                        X.wait(v, f"wqsp{hh}_{pu}")
                lu = ORD[u]
                tc, ntc = (T1C, NT1C) if lu < NQU else (T2C, NT2C)
                X.chain(v, "cV", v.tensor_scalar(out=p1tv4[ptb(u)], in0=p1s[u % 3],
                                                 scalar1=statc[:, ntc:ntc + 1],
                                                 scalar2=None, op0=ALU.is_le))
                X.bump("cV", v.scalar_tensor_tensor(out=p1tv4[ptb(u)], in0=p1s[u % 3],
                                                    scalar=statc[:, tc:tc + 1],
                                                    in1=p1tv4[ptb(u)],
                                                    op0=ALU.is_ge, op1=ALU.subtract))
                X.mark(f"wqf_{u}", "cV")
                QS["q"] += 1

        def p3_sync(q):
            X.wait(q, "qkv_spills")
            X.wait(q, "v_all")
            X.wait(q, "bnexp")
            X.dma("mskq", q.dma_start(out=msk, in_=masks))
            X.mark("msk_ld", "mskq")
            def kqld(h):
                if h >= 2:
                    X.wait(q, f"head_done_{h-2}")
                X.dma(f"kq{h % 2}", q.dma_start(out=kTb[h % 2][0:64, :],
                                                in_=k_hbm[h * 128:h * 128 + 64, :]))
                X.dma(f"kq{h % 2}", q.dma_start(out=kTb[h % 2][64:128, :],
                                                in_=k_hbm[h * 128 + 64:(h + 1) * 128, :]))
                X.dma(f"kq{h % 2}", q.dma_start(out=qTb[h % 2], in_=q_hbm[h * 128:(h + 1) * 128, :]))
                X.mark(f"kq_ld_{h}", f"kq{h % 2}")
            kqld(0)
            kqld(1)
            for h in range(H):
                if h + 2 < H:
                    kqld(h + 2)
                for _k in range(6):
                    qld_issue(q, 2)
                    qsp_issue(q, 2)

        def p3_tensor(t):
            def thr_w1(t):
                X.wait(t, "thrv_w1_a")
                inst = t.matmul(psA[0:1, 0:1], cst[:, 0:1], statc[:, 32:33],
                                start=True, stop=True)
                inst = t.matmul(psA[0:1, 1:2], cst[:, 0:1], statc[:, 33:34],
                                start=True, stop=True)
                X.bump("cT", inst)
                X.mark("thrt_w1_a", "cT")
                X.wait(t, "thrv_w1_b")
                inst = t.matmul(psA[:, 0:1], ones32r, srow[0:1, 18:19], start=True, stop=True)
                inst = t.matmul(psA[:, 1:2], ones32r, srow[0:1, 19:20], start=True, stop=True)
                X.bump("cT", inst)
                X.mark("thrt_w1_b", "cT")
                X.wait(t, "thrcp_w1")

            def thr_w2(t):
                X.wait(t, "thrv_w2_a")
                inst = t.matmul(psA[0:1, 0:1], cst[:, 0:1], statc[:, 34:35],
                                start=True, stop=True)
                inst = t.matmul(psA[0:1, 1:2], cst[:, 0:1], statc[:, 35:36],
                                start=True, stop=True)
                X.bump("cT", inst)
                X.mark("thrt_w2_a", "cT")
                X.wait(t, "thrv_w2_b")
                inst = t.matmul(psA[:, 0:1], ones32r, srow[0:1, 18:19], start=True, stop=True)
                inst = t.matmul(psA[:, 1:2], ones32r, srow[0:1, 19:20], start=True, stop=True)
                X.bump("cT", inst)
                X.mark("thrt_w2_b", "cT")
                X.wait(t, "thrcp_w2")

            def emit_sc(t, it):
                h, j, ci, c0, n, K, kl = it
                gch = gchunk_of[(h, j, ci)]
                par = gch % 2
                psS = psA if par == 0 else psB
                if j == 0 and ci == 0:
                    X.wait(t, f"kq_ld_{h}")
                if gch >= 2:
                    X.wait(t, f"exp_{gch-2}")
                inst = None
                for u in range(n):
                    s_ = kl[c0 + u]
                    inst = t.matmul(psS[:, u * 256:(u + 1) * 256],
                                    kTb[h % 2][:, s_ * 128:(s_ + 1) * 128],
                                    qTb[h % 2][:, j * 256:(j + 1) * 256],
                                    start=True, stop=True)
                X.bump("cT", inst)
                X.mark(f"sc_{gch}", "cT")

            def emit_av(t, it):
                h, j, ci, c0, n, K, kl = it
                gch = gchunk_of[(h, j, ci)]
                par = gch % 2
                X.wait(t, f"exp_{gch}")
                if (h, j, ci) in mask_ev:
                    X.wait(t, mask_ev[(h, j, ci)])
                if ci == 0:
                    X.wait(t, f"aoev_{h}_{j-1}" if j > 0 else
                           (f"aoev_{h-1}_{NPAIR-1}" if h > 0 else "negc_all"))
                inst = None
                for u in range(n):
                    pos = c0 + u
                    s_ = kl[pos]
                    for qt in range(2):
                        pav = psA if qt == 0 else psB
                        inst = t.matmul(pav[:, 1536:1665],
                                        PTv[par][:, u * 256 + qt * 128:u * 256 + qt * 128 + 128],
                                        vslot(h, s_),
                                        start=(pos == 0),
                                        stop=(pos == K - 1))
                X.bump("cT", inst)
                X.mark(f"av_{gch}", "cT")
                if j == NPAIR - 1 and ci == nchunk_pair[j] - 1:
                    X.mark(f"head_done_{h}", "cT")

            X.wait(t, "negc_all")
            thr_w1(t)
            thr_w2(t)
            SKEW = 2
            seq = []
            for h in range(H):
                for j in range(NPAIR):
                    K = KJ[j]
                    kl = KLIST[K]
                    for ci, (c0, n) in enumerate(sched[j]):
                        seq.append((h, j, ci, c0, n, K, kl))
            for idx, it in enumerate(seq):
                emit_sc(t, it)
                if idx >= SKEW:
                    emit_av(t, seq[idx - SKEW])
            for it in seq[-SKEW:]:
                emit_av(t, it)
            X.mark("attn_done", "cT")

        def p3_scalar(s):
            for h in range(H):
                for j in range(NPAIR):
                    for ci, (c0, n) in enumerate(sched[j]):
                        gch = gchunk_of[(h, j, ci)]
                        par = gch % 2
                        psS = psA if par == 0 else psB
                        X.wait(s, f"sc_{gch}")
                        if gch >= 2:
                            X.wait(s, f"av_{gch-2}")
                        X.bump("cS", s.activation(out=PTv[par][:, 0:n * 256],
                                                  in_=psS[:, 0:n * 256], func=AF.Exp,
                                                  bias=statc[:, 8 + h:9 + h]))
                        X.mark(f"exp_{gch}", "cS")
            X.mark("attn_s_done", "cS")

        mask_ev = {}

        srowI = srow.bitcast(I32)

        def emit_thr_dekker(v):
            sh = psA[0:1, 0:1]
            sl = psA[0:1, 1:2]
            c = lambda i: srow[0:1, i:i + 1]
            ci = lambda i: srowI[0:1, i:i + 1]
            ch = X.chain
            ch(v, "cV", v.tensor_scalar(out=c(8), in0=sh, scalar1=2.0 ** -12, scalar2=SPL_C,
                                        op0=ALU.mult, op1=ALU.add))
            ch(v, "cV", v.tensor_scalar(out=c(8), in0=c(8), scalar1=-SPL_C, scalar2=4096.0,
                                        op0=ALU.add, op1=ALU.mult))
            ch(v, "cV", v.scalar_tensor_tensor(out=c(10), in0=c(8), scalar=-1.0, in1=sh,
                                               op0=ALU.mult, op1=ALU.add))
            ch(v, "cV", v.tensor_scalar(out=c(11), in0=sh, scalar1=CH, scalar2=None,
                                        op0=ALU.mult))
            ch(v, "cV", v.tensor_scalar(out=c(15), in0=c(8), scalar1=CH_A, scalar2=None,
                                        op0=ALU.mult))
            ch(v, "cV", v.tensor_sub(out=c(12), in0=c(15), in1=c(11)))
            for src, k in ((8, CH_B), (10, CH_A), (10, CH_B)):
                ch(v, "cV", v.tensor_scalar(out=c(15), in0=c(src), scalar1=k, scalar2=None,
                                            op0=ALU.mult))
                ch(v, "cV", v.tensor_add(out=c(12), in0=c(12), in1=c(15)))
            ch(v, "cV", v.tensor_scalar(out=c(15), in0=sh, scalar1=CL, scalar2=None,
                                        op0=ALU.mult))
            ch(v, "cV", v.tensor_add(out=c(12), in0=c(12), in1=c(15)))
            ch(v, "cV", v.tensor_scalar(out=c(15), in0=sl, scalar1=CH, scalar2=None,
                                        op0=ALU.mult))
            ch(v, "cV", v.tensor_add(out=c(12), in0=c(12), in1=c(15)))
            ch(v, "cV", v.tensor_add(out=c(13), in0=c(11), in1=c(12)))
            ch(v, "cV", v.tensor_sub(out=c(15), in0=c(13), in1=c(11)))
            ch(v, "cV", v.tensor_sub(out=c(14), in0=c(12), in1=c(15)))
            ch(v, "cV", v.tensor_scalar(out=c(15), in0=c(14), scalar1=0.0, scalar2=None,
                                        op0=ALU.is_gt))
            ch(v, "cV", v.tensor_scalar(out=c(16), in0=c(13), scalar1=ULP_PUSH, scalar2=None,
                                        op0=ALU.mult))
            ch(v, "cV", v.tensor_mul(out=c(16), in0=c(16), in1=c(15)))
            ch(v, "cV", v.tensor_add(out=c(18), in0=c(13), in1=c(16)))
            ch(v, "cV", v.tensor_scalar(out=c(19), in0=c(18), scalar1=-1.0, scalar2=None,
                                        op0=ALU.mult))

        def p3_vec(v):
            X.wait(v, "p1_all")
            for lv in [64, 32, 16, 8, 4, 2, 1]:
                X.chain(v, "cV", v.tensor_add(out=pacc[:, 0:lv], in0=pacc[:, 0:lv],
                                              in1=pacc[:, lv:2 * lv]))
            X.chain(v, "cV", v.tensor_scalar(out=statc[:, 36:37], in0=pacc[:, 0:1],
                                             scalar1=RND_C, scalar2=None, op0=ALU.add))
            X.chain(v, "cV", v.tensor_scalar(out=statc[:, 32:33], in0=statc[:, 36:37],
                                             scalar1=-RND_C, scalar2=None, op0=ALU.add))
            X.chain(v, "cV", v.tensor_sub(out=statc[:, 33:34], in0=pacc[:, 0:1],
                                          in1=statc[:, 32:33]))
            X.mark("thrv_w1_a", "cV")
            X.wait(v, "thrt_w1_a")
            emit_thr_dekker(v)
            X.mark("thrv_w1_b", "cV")
            X.wait(v, "thrt_w1_b")
            X.chain(v, "cV", v.tensor_copy(out=statc[:, T1C:T1C + 1], in_=psA[:, 0:1]))
            X.chain(v, "cV", v.tensor_copy(out=statc[:, NT1C:NT1C + 1], in_=psA[:, 1:2]))
            X.mark("thrcp_w1", "cV")
            for lv in [64, 32, 16, 8, 4, 2, 1]:
                X.chain(v, "cV", v.tensor_add(out=pacc[:, 128:128 + lv],
                                              in0=pacc[:, 128:128 + lv],
                                              in1=pacc[:, 128 + lv:128 + 2 * lv]))
            X.chain(v, "cV", v.tensor_scalar(out=statc[:, 36:37], in0=pacc[:, 128:129],
                                             scalar1=RND_C, scalar2=None, op0=ALU.add))
            X.chain(v, "cV", v.tensor_scalar(out=statc[:, 34:35], in0=statc[:, 36:37],
                                             scalar1=-RND_C, scalar2=None, op0=ALU.add))
            X.chain(v, "cV", v.tensor_sub(out=statc[:, 35:36], in0=pacc[:, 128:129],
                                          in1=statc[:, 34:35]))
            X.mark("thrv_w2_a", "cV")
            X.wait(v, "thrt_w2_a")
            emit_thr_dekker(v)
            X.mark("thrv_w2_b", "cV")
            X.wait(v, "thrt_w2_b")
            X.chain(v, "cV", v.tensor_copy(out=statc[:, T2C:T2C + 1], in_=psA[:, 0:1]))
            X.chain(v, "cV", v.tensor_copy(out=statc[:, NT2C:NT2C + 1], in_=psA[:, 1:2]))
            X.mark("thrcp_w2", "cV")
            for h in range(H):
                for j in range(NPAIR):
                    K = KJ[j]
                    kl_j = KLIST[K]
                    mpos = [kl_j.index(L) for L in MLOC[j]]
                    for ci, (c0, n) in enumerate(sched[j]):
                        runs = []
                        for si, p_ in enumerate(mpos):
                            if c0 <= p_ < c0 + n:
                                if runs and runs[-1][1] == p_ and runs[-1][3] == si:
                                    runs[-1] = (runs[-1][0], p_ + 1, runs[-1][2], si + 1)
                                else:
                                    runs.append((p_, p_ + 1, si, si + 1))
                        if not runs:
                            continue
                        gch = gchunk_of[(h, j, ci)]
                        par = gch % 2
                        if h == 0:
                            X.wait(v, "msk_ld")
                        X.wait(v, f"exp_{gch}")
                        for (p0, p1, s0, s1) in runs:
                            X.bump("cV", v.tensor_mul(
                                out=PTv[par][:, (p0 - c0) * 256:(p1 - c0) * 256],
                                in0=PTv[par][:, (p0 - c0) * 256:(p1 - c0) * 256],
                                in1=msk[:, (j * 4 + s0) * 256:(j * 4 + s1) * 256]))
                        ev = f"mskd_{h}_{j}_{ci}"
                        X.mark(ev, "cV")
                        if X.dry:
                            mask_ev[(h, j, ci)] = ev
                    lastg = gchunk_of[(h, j, nchunk_pair[j] - 1)]
                    X.wait(v, f"av_{lastg}")
                    X.chain(v, "cV", v.reciprocal(out=statc[:, RECC:RECC + 1], in_=psA[:, 1664:1665]))
                    X.bump("cV", v.tensor_scalar(out=ao_tm(2 * j)[:, h * 128:(h + 1) * 128],
                                                 in0=psA[:, 1536:1664],
                                                 scalar1=statc[:, RECC:RECC + 1], scalar2=None,
                                                 op0=ALU.mult))
                    X.chain(v, "cV", v.reciprocal(out=statc[:, RECC + 1:RECC + 2], in_=psB[:, 1664:1665]))
                    X.bump("cV", v.tensor_scalar(out=ao_tm(2 * j + 1)[:, h * 128:(h + 1) * 128],
                                                 in0=psB[:, 1536:1664],
                                                 scalar1=statc[:, RECC + 1:RECC + 2], scalar2=None,
                                                 op0=ALU.mult))
                    X.mark(f"aoev_{h}_{j}", "cV")
                quant(v, 12)
            X.mark("attn_v_done", "cV")

        def p4_scalar(s):
            X.wait(s, "attn_v_done")
            for t8 in range(8):
                X.dma("aoq", s.dma_start(out=ao_hbm[t8 * 128:(t8 + 1) * 128, :],
                                        in_=A16[:, t8 * 2048:(t8 + 1) * 2048]))
            X.mark("ao_spill", "aoq")
            X.wait(s, "ao_spill")
            X.wait(s, "attn_done")
            for cc in range(16):
                X.dma("aoq", s.dma_start(out=ao_fm(cc), in_=ao_hbm[:, cc * 128:(cc + 1) * 128],
                                        transpose=True))
            X.mark("ao_fm", "aoq")
            for sw in range(8):
                X.wait(s, f"qgrp_0_{sw}")
            for dc in range(16):
                X.dma("w1x", s.dma_start(
                    out=w1qTv(dc),
                    in_=w1q_hbm[0:GF, dc * 128:(dc + 1) * 128],
                    transpose=True))
            X.mark("w1x_0", "w1x")

        def p4_sync(q):
            X.wait(q, "ao_spill")
            for k4 in range(4):
                X.dma("x2q", q.dma_start(
                    out=A32[:, k4 * 4096:(k4 + 1) * 4096].rearrange("p (tt d) -> p tt d", d=2048),
                    in_=xg[k4 * 256:(k4 + 1) * 256, :].rearrange("(tt p) d -> p tt d", p=128)))
            X.mark("x2_init", "x2q")
            while QS["ld"] < 2 * NQU or QS["sp"] < 2 * NQU:
                qld_issue(q, 4)
                qsp_issue(q, 4)
            for b_ in range(4):
                X.mark(f"lqfin_{b_}", f"lq{b_}")
            for sw in range(8):
                X.mark(f"wsfin_{sw}", f"ws{sw}")

        def p4_pool(gp):
            X.wait(gp, "attn_done")
            for hh in range(2):
                X.wait(gp, f"wqsp{hh}_{NQIL - 2}")
                X.wait(gp, f"wqsp{hh}_{NQIL - 1}")
            for db in range(4):
                if db >= 2:
                    X.wait(gp, f"op_mm_{db-2}_7")
                for cc in range(16):
                    X.dma(f"wo{db % 2}", gp.dma_start(out=woutst[db % 2][:, cc * 512:(cc + 1) * 512],
                                             in_=wout_r[:, cc, db * 512:(db + 1) * 512]))
                X.mark(f"wout_{db}", f"wo{db % 2}")

        def p4_tensor(t):
            def pov(tt):
                ps = psA if tt % 8 < 4 else psB
                return ps[:, (tt % 4) * 512:(tt % 4) * 512 + 512]
            X.wait(t, "attn_s_done")
            X.wait(t, "attn_v_done")
            X.wait(t, "ao_fm")
            for db in range(4):
                X.wait(t, f"wout_{db}")
                if db >= 1:
                    for tt in range(8):
                        X.wait(t, f"op_ev_{(db - 1) * 8 + tt}")
                for cc in range(16):
                    for tt in range(8):
                        t.matmul(pov(tt), ao_fm(cc)[:, tt * 128:(tt + 1) * 128],
                                 woutst[db % 2][:, cc * 512:(cc + 1) * 512],
                                 start=(cc == 0), stop=False)
                X.wait(t, "init_pool")
                for tt in range(8):
                    inst = t.matmul(pov(tt), onesb[0:1, :],
                                    brows[0:1, 2048 + db * 512:2560 + db * 512],
                                    start=False, stop=True)
                    X.bump("cT", inst)
                    X.mark(f"op_mm_{db}_{tt}", "cT")
            X.mark("op_done", "cT")

        def p4_vec(v):
            X.wait(v, "x2_init")
            for db in range(4):
                for tt in range(8):
                    u = db * 8 + tt
                    bank = u % 8
                    ps = psA if bank < 4 else psB
                    po = ps[:, (bank % 4) * 512:(bank % 4) * 512 + 512]
                    X.wait(v, f"op_mm_{db}_{tt}")
                    X.bump("cV", v.tensor_add(out=x2v(tt)[:, db * 512:(db + 1) * 512],
                                              in0=po, in1=x2v(tt)[:, db * 512:(db + 1) * 512]))
                    X.mark(f"op_ev_{u}", "cV")
                    quant(v, 2)
            X.mark("op_v_done", "cV")

        def p5_scalar(s):
            X.wait(s, "op_done")
            for sw in range(8):
                X.wait(s, f"qgrp_0_{sw}")
            for fc in range(8):
                X.dma("w2x", s.dma_start(
                    out=w2qTv(fc),
                    in_=w2q_hbm[:, fc * 128:(fc + 1) * 128],
                    transpose=True))
            X.mark("w2x_0", "w2x")
            for i in range(8):
                X.wait(s, f"op_ev_{24+i}")
                if i >= 1:
                    X.wait(s, f"ln2_var_{i-1}")
                X.chain(s, "cS", s.activation(out=ln2sq, in_=x2v(i), func=AF.Identity,
                                              accum_out=statc[:, LN_S0:LN_S0 + 1]))
                X.bump("cS", s.activation(out=ln2sq, in_=x2v(i), func=AF.Square,
                                          accum_out=statc[:, LN_S2:LN_S2 + 1]))
                X.mark(f"ln2_sums_{i}", "cS")
                X.wait(s, f"ln2_var_{i}")
                X.chain(s, "cS", s.activation(out=statc[:, LN_RSTD:LN_RSTD + 1],
                                              in_=statc[:, LN_VAR:LN_VAR + 1], func=AF.Ln,
                                              bias=cst[:, 1:2]))
                X.chain(s, "cS", s.activation(out=statc[:, LN_RSTD:LN_RSTD + 1],
                                              in_=statc[:, LN_RSTD:LN_RSTD + 1], func=AF.Exp,
                                              scale=-0.5))
                X.mark(f"ln2_rstd_{i}", "cS")
                X.wait(s, f"ln2_nmr_{i}")
                if i >= 1:
                    X.wait(s, f"ln2_sp_{i-1}")
                X.chain(s, "cS", s.activation(out=x2st,
                                              in_=x2v(i), func=AF.Identity,
                                              scale=statc[:, LN_RSTD:LN_RSTD + 1],
                                              bias=statc[:, LN_NMR:LN_NMR + 1]))
                X.mark(f"ln2_norm_{i}", "cS")
                X.dma("l2q", s.dma_start(out=x2ln_hbm[i * 128:(i + 1) * 128, :],
                                        in_=x2st))
                X.mark(f"ln2_sp_{i}", "l2q")
            X.mark("ln2_spills", "l2q")

        def p5_vec(v):
            for i in range(8):
                X.wait(v, f"ln2_sums_{i}")
                X.chain(v, "cV", v.tensor_scalar(out=statc[:, LN_MEAN:LN_MEAN + 1],
                                                 in0=statc[:, LN_S0:LN_S0 + 1], scalar1=INV_D,
                                                 scalar2=None, op0=ALU.mult))
                X.chain(v, "cV", v.tensor_scalar(out=statc[:, LN_E2:LN_E2 + 1],
                                                 in0=statc[:, LN_S2:LN_S2 + 1], scalar1=INV_D,
                                                 scalar2=None, op0=ALU.mult))
                X.chain(v, "cV", v.tensor_mul(out=statc[:, LN_VAR:LN_VAR + 1],
                                              in0=statc[:, LN_MEAN:LN_MEAN + 1],
                                              in1=statc[:, LN_MEAN:LN_MEAN + 1]))
                X.chain(v, "cV", v.tensor_sub(out=statc[:, LN_VAR:LN_VAR + 1],
                                              in0=statc[:, LN_E2:LN_E2 + 1],
                                              in1=statc[:, LN_VAR:LN_VAR + 1]))
                X.mark(f"ln2_var_{i}", "cV")
                X.wait(v, f"ln2_rstd_{i}")
                X.chain(v, "cV", v.tensor_scalar(out=statc[:, LN_NMR:LN_NMR + 1],
                                                 in0=statc[:, LN_MEAN:LN_MEAN + 1],
                                                 scalar1=statc[:, LN_RSTD:LN_RSTD + 1],
                                                 scalar2=-1.0, op0=ALU.mult, op1=ALU.mult))
                X.mark(f"ln2_nmr_{i}", "cV")

        def p6_scalar(s):
            for g in range(NGRP):
                for fc in range(8):
                    X.wait(s, f"f1_mm_{g}_{fc}")
                    if g > 0 and fc == 0:
                        X.wait(s, f"f2_mm_{g-1}_7")
                    for t2 in range(2):
                        bank = (2 * fc + t2) % 4
                        X.bump("cS", s.activation(
                            out=hgv(fc)[:, t2 * 512:(t2 + 1) * 512],
                            in_=psA[:, bank * 512:(bank + 1) * 512],
                            func=AF.Gelu, bias=b1cols[:, g * 8 + fc:g * 8 + fc + 1]))
                    X.mark(f"f1gelu_{g}_{fc}", "cS")

            for tb in range(8):
                X.wait(s, f"f2ev_{NGRP-1}_{tb}")
                X.dma("yq", s.dma_start(out=y[tb * 128:(tb + 1) * 128, :], in_=x2v(tb)))
            X.mark("y_stores", "yq")

        def p6_sync(q):
            X.wait(q, "ln2_spills")
            X.wait(q, "op_done")
            X.wait(q, "w1x_0")
            X.wait(q, "w2x_0")
            for dc in range(16):
                X.dma("l2qb", q.dma_start(out=x2lnfm(dc),
                                         in_=x2ln_hbm[:, dc * 128:(dc + 1) * 128],
                                         transpose=True))
            X.mark("x2ln_fm", "l2qb")
            for g in range(NGRP):
                if g >= 1:
                    X.wait(q, f"f2_mm_{g-1}_7")
                    for sw in range(8):
                        X.wait(q, f"qgrp_{g}_{sw}")
                    for fc in range(8):
                        X.dma("w2x", q.dma_start(
                            out=w2qTv(fc),
                            in_=w2q_hbm[:, g * GF + fc * 128:g * GF + (fc + 1) * 128],
                            transpose=True))
                    X.mark(f"w2x_{g}", "w2x")
                if g + 1 < NGRP:
                    X.wait(q, f"f1_mm_{g}_7")
                    for sw in range(8):
                        X.wait(q, f"qgrp_{g+1}_{sw}")
                    for dc in range(16):
                        X.dma("w1xb", q.dma_start(
                            out=w1qTv(dc),
                            in_=w1q_hbm[(g + 1) * GF:(g + 2) * GF, dc * 128:(dc + 1) * 128],
                            transpose=True))
                    X.mark(f"w1x_{g+1}", "w1xb")

        def p6_tensor(t):
            X.wait(t, "x2ln_fm")
            for g in range(NGRP):
                X.wait(t, f"w1x_{g}")
                for fc in range(8):
                    if g > 0 or fc >= 2:
                        pg, pf = (g, fc - 2) if fc >= 2 else (g - 1, fc + 6)
                        X.wait(t, f"f1gelu_{pg}_{pf}")
                    if g > 0 and fc < 2:
                        X.wait(t, f"f2ev_{g-1}_7")
                    inst = None
                    for dc in range(16):
                        lhs = w1qTv(dc)[:, fc * 128:(fc + 1) * 128]
                        for t2 in range(2):
                            bank = (2 * fc + t2) % 4
                            inst = t.matmul(psA[:, bank * 512:(bank + 1) * 512], lhs,
                                            x2lnfm(dc)[:, t2 * 512:(t2 + 1) * 512],
                                            start=(dc == 0), stop=(dc == 15))
                    X.bump("cT", inst)
                    X.mark(f"f1_mm_{g}_{fc}", "cT")
                X.wait(t, f"w2x_{g}")
                X.wait(t, f"f1gelu_{g}_7")
                for tb in range(8):
                    ps = psB if tb % 2 == 0 else psA
                    if g > 0 or tb >= 2:
                        pg, pt = (g, tb - 2) if tb >= 2 else (g - 1, tb + 6)
                        X.wait(t, f"f2ev_{pg}_{pt}")
                    inst = None
                    for fc in range(8):
                        lhs = hgv(fc)[:, tb * 128:(tb + 1) * 128]
                        for db in range(4):
                            inst = t.matmul(ps[:, db * 512:(db + 1) * 512], lhs,
                                            w2qTv(fc)[:, db * 512:(db + 1) * 512],
                                            start=(fc == 0),
                                            stop=(fc == 7 and g > 0))
                    if g == 0:
                        for db in range(4):
                            inst = t.matmul(ps[:, db * 512:(db + 1) * 512], onesb[0:1, :],
                                            brows[0:1, 4096 + db * 512:4608 + db * 512],
                                            start=False, stop=True)
                    X.bump("cT", inst)
                    X.mark(f"f2_mm_{g}_{tb}", "cT")
            X.mark("ffn_done", "cT")

        def p6_vec(v):
            for g in range(NGRP):
                for tb in range(8):
                    ps = psB if tb % 2 == 0 else psA
                    X.wait(v, f"f2_mm_{g}_{tb}")
                    for db in range(4):
                        X.bump("cV", v.tensor_add(out=x2v(tb)[:, db * 512:(db + 1) * 512],
                                                  in0=ps[:, db * 512:(db + 1) * 512],
                                                  in1=x2v(tb)[:, db * 512:(db + 1) * 512]))
                    X.mark(f"f2ev_{g}_{tb}", "cV")
            quant(v, 2 * NQU)
            X.mark("ffn_v_done", "cV")

        def p7_sync(q):
            X.wait(q, "y_stores")
            X.wait(q, "ffn_done")
            X.wait(q, "ffn_v_done")
            for b_ in range(4):
                X.wait(q, f"lqfin_{b_}")
            for sw in range(8):
                X.wait(q, f"wsfin_{sw}")

        PHASES = [
            (p0_sync, "sync"), (p0_pool, "gpsimd"), (p0_vec, "vector"),
            (p1_sync, "sync"), (p1_scalar, "scalar"), (p1_vec, "vector"),
            (p2_sync, "sync"), (p2_tensor, "tensor"),
            (p2_vec, "vector"), (p2_vec_b, "vector"), (p2_scalar, "scalar"),
            (p2c_sync, "sync"), (p2c_scalar, "scalar"), (p2c_tensor, "tensor"), (p2c_vec, "vector"),
            (p3_sync, "sync"), (p3_tensor, "tensor"),
            (p3_scalar, "scalar"), (p3_vec, "vector"),
            (p4_scalar, "scalar"), (p4_sync, "sync"), (p4_pool, "gpsimd"),
            (p4_tensor, "tensor"), (p4_vec, "vector"),
            (p5_scalar, "scalar"), (p5_vec, "vector"),
            (p6_scalar, "scalar"), (p6_sync, "sync"),
            (p6_tensor, "tensor"), (p6_vec, "vector"),
            (p7_sync, "sync"),
        ]

        for fn, _ in PHASES:
            fn(_DryEng())
        X.reset()
        QS["ld"] = QS["sp"] = QS["q"] = QS["g"] = 0
        block = stk.enter_context(nc.Block())
        for fn, engname in PHASES:
            dec = getattr(block, engname)

            def _emit(eng, _fn=fn):
                _fn(eng)
            dec(_emit)

    return nc


_NC_CACHE = {}


def _perm(half):
    own = PAIRS_A if half == 0 else PAIRS_B
    par = PAIRS_B if half == 0 else PAIRS_A
    rows = []
    for ga, gb in own + par:
        rows.extend(range(ga * 128, ga * 128 + 128))
        rows.extend(range(gb * 128, gb * 128 + 128))
    return np.array(rows, dtype=np.int64)


def _build_masks(half):
    own = PAIRS_A if half == 0 else PAIRS_B
    par = PAIRS_B if half == 0 else PAIRS_A
    m = np.zeros((128, 4096), dtype=np.float32)
    p = np.arange(128)[:, None]
    qq = np.arange(128)[None, :]
    for j in range(NPAIR):
        for sidx in range(4):
            gk = own[j][sidx] if sidx < 2 else par[j][sidx - 2]
            kpos = gk * 128 + p
            base = (j * 4 + sidx) * 256
            m[:, base:base + 128] = (kpos <= own[j][0] * 128 + qq)
            m[:, base + 128:base + 256] = (kpos <= own[j][1] * 128 + qq)
    return m.astype(BF)


def kernel(L0_attn_x, L0_attn_w_qkv, L0_attn_b_qkv, L0_attn_w_out, L0_attn_b_out,
           L0_ffn_w1, L0_ffn_b1, L0_ffn_w2, L0_ffn_b2):
    x = np.ascontiguousarray(np.asarray(L0_attn_x, dtype=np.float32))
    wqkv = np.ascontiguousarray(np.asarray(L0_attn_w_qkv, dtype=np.float32))
    bqkv = np.ascontiguousarray(np.asarray(L0_attn_b_qkv, dtype=np.float32))
    wout = np.ascontiguousarray(np.asarray(L0_attn_w_out, dtype=np.float32))
    bout = np.ascontiguousarray(np.asarray(L0_attn_b_out, dtype=np.float32))
    w1 = np.ascontiguousarray(np.asarray(L0_ffn_w1, dtype=np.float32))
    b1 = np.ascontiguousarray(np.asarray(L0_ffn_b1, dtype=np.float32))
    w2 = np.ascontiguousarray(np.asarray(L0_ffn_w2, dtype=np.float32))
    b2 = np.ascontiguousarray(np.asarray(L0_ffn_b2, dtype=np.float32))
    B = x.shape[0]

    if "nc" not in _NC_CACHE:
        _NC_CACHE["nc"] = build_program()
    nc = _NC_CACHE["nc"]

    in_maps = []
    meta = []
    for c in range(8):
        b, half = c // 2, c % 2
        perm = _perm(half)
        meta.append((b, perm))
        in_maps.append({
            "xg": np.ascontiguousarray(x[b][perm]),
            "masks": _build_masks(half),
            "wqkv": wqkv, "bqkv": bqkv, "wout": wout, "bout": bout,
            "w1": w1, "b1": b1, "w2": w2, "b2": b2,
        })

    res = run_bass_kernel_spmd(nc, in_maps, core_ids=list(range(8)))
    out = np.empty((B, T, D), dtype=np.float32)
    for c in range(8):
        b, perm = meta[c]
        out[b][perm[:TO]] = res.results[c]["y"]
    return out


# version 08
# speedup vs baseline: 1.4616x; 1.4616x over previous
import contextlib
import numpy as np
import ml_dtypes
import concourse.bass as bass
import concourse.mybir as mybir
from concourse.bass_utils import run_bass_kernel_spmd

F32 = mybir.dt.float32
BF16 = mybir.dt.bfloat16
I32 = mybir.dt.int32
AF = mybir.ActivationFunctionType
ALU = mybir.AluOpType
AX = mybir.AxisListType
BF = ml_dtypes.bfloat16

SEMS = ["cT", "cS", "cV", "cG",
        "iniP", "lx0", "lx1", "xs0", "xs1", "xlx", "qw0", "qw1",
        "lq0", "lq1", "lq2", "lq3",
        "ws0", "ws1", "ws2", "ws3", "ws4", "ws5", "ws6", "ws7",
        "l2qb", "w1xb", "aoq2",
        "vw", "qsp", "mskq", "kq0", "kq1",
        "aoq", "x2q", "wo0", "wo1", "l2q", "w1x", "w2x", "yq"]

D = 2048
T = 2048
TO = 1024
H = 16
NT = 16
KJ = [16, 12, 8, 4]
NPAIR = 4
PAIRS_A = [(14, 15), (8, 9), (6, 7), (0, 1)]
PAIRS_B = [(12, 13), (10, 11), (4, 5), (2, 3)]
QSCALE = 1.0 / float(np.sqrt(128.0))
INV_D = 1.0 / D
EPS = 1e-5
NGRP = 8
GF = 1024
NQU = 128
ORD = []
for _g in range(NGRP):
    ORD += [16 * _g + _k for _k in range(16)]
    ORD += [NQU + _r * 8 + _g for _r in range(16)]
assert sorted(ORD) == list(range(2 * NQU))
_f = np.float32
_CHD = 0.7 / 16777216.0
CH = float(_f(_CHD))
CL = float(_f(_CHD - CH))
_c = _f(_f(CH) * _f(4097.0))
CH_A = float(_f(_c - _f(_c - _f(CH))))
CH_B = float(_f(_f(CH) - _f(CH_A)))
RND_C = 12582912.0
SPL_C = 12582912.0
ULP_PUSH = float(_f(2.0 ** -24 * 1.001))


KLIST = {
    16: list(range(16)),
    12: [2, 3, 4, 5, 6, 7, 10, 11, 12, 13, 14, 15],
    8: [4, 5, 6, 7, 12, 13, 14, 15],
    4: [6, 7, 14, 15],
}
MLOC = {j: [2 * j, 2 * j + 1, 8 + 2 * j, 9 + 2 * j] for j in range(4)}


def _chunks(n, c=6):
    out, i = [], 0
    while i < n:
        out.append((i, min(c, n - i)))
        i += c
    return out


class _DryInst:
    def then_inc(self, sem, k):
        return self


class _DryReg:
    def __enter__(self):
        return self

    def __exit__(self, *a):
        return False


class _DryEng:
    def register(self, *a, **k):
        return _DryReg()

    def snap(self, *a, **k):
        return None

    def wait_ge(self, *a, **k):
        return None

    def __getattr__(self, name):
        def f(*a, **k):
            return _DryInst()
        return f


class Ctx:
    def __init__(self):
        self.dry = True
        self.ctr = {n: 0 for n in SEMS}
        self.ev = {}
        self.sem = {}

    def reset(self):
        self.dry = False
        for k in self.ctr:
            self.ctr[k] = 0

    def bump(self, cname, inst, k=1):
        if not self.dry:
            inst.then_inc(self.sem[cname], k)
        self.ctr[cname] += k
        return self.ctr[cname]

    def dma(self, cname, inst):
        return self.bump(cname, inst, 16)

    def mark(self, name, cname):
        if self.dry:
            assert name not in self.ev, f"duplicate event {name}"
            self.ev[name] = (cname, self.ctr[cname])
        else:
            assert self.ev[name] == (cname, self.ctr[cname]), \
                f"pass mismatch at {name}: dry {self.ev[name]} real {(cname, self.ctr[cname])}"

    def wait(self, eng, name):
        if self.dry:
            return
        cname, v = self.ev[name]
        if v > 0:
            eng.wait_ge(self.sem[cname], v)

    def chain(self, eng, cname, inst):
        n = self.bump(cname, inst)
        if not self.dry:
            eng.wait_ge(self.sem[cname], n)
        return n


def build_program(debug=()):
    nc = bass.Bass("TRN2", target_bir_lowering=False)

    def dk(name):
        return "ExternalOutput" if name in debug else "Internal"

    xg = nc.dram_tensor("xg", [T, D], F32, kind="ExternalInput").ap()
    masks = nc.dram_tensor("masks", [128, 4096], BF16, kind="ExternalInput").ap()
    wqkv = nc.dram_tensor("wqkv", [D, 3 * D], F32, kind="ExternalInput").ap()
    bqkv = nc.dram_tensor("bqkv", [3 * D], F32, kind="ExternalInput").ap()
    wout = nc.dram_tensor("wout", [D, D], F32, kind="ExternalInput").ap()
    bout = nc.dram_tensor("bout", [D], F32, kind="ExternalInput").ap()
    w1 = nc.dram_tensor("w1", [8192, D], F32, kind="ExternalInput").ap()
    b1 = nc.dram_tensor("b1", [8192], F32, kind="ExternalInput").ap()
    w2 = nc.dram_tensor("w2", [D, 8192], F32, kind="ExternalInput").ap()
    b2 = nc.dram_tensor("b2", [D], F32, kind="ExternalInput").ap()
    y = nc.dram_tensor("y", [TO, D], F32, kind="ExternalOutput").ap()

    xln_hbm = nc.dram_tensor("xln_hbm", [T, D], BF16, kind=dk("xln_hbm")).ap()
    q_hbm = nc.dram_tensor("q_hbm", [D, TO], BF16, kind=dk("q_hbm")).ap()
    k_hbm = nc.dram_tensor("k_hbm", [D, T], BF16, kind=dk("k_hbm")).ap()
    ao_hbm = nc.dram_tensor("ao_hbm", [TO, D], BF16, kind=dk("ao_hbm")).ap()
    x2ln_hbm = nc.dram_tensor("x2ln_hbm", [TO, D], BF16, kind=dk("x2ln_hbm")).ap()
    w1q_hbm = nc.dram_tensor("w1q_hbm", [8192, D], BF16, kind=dk("w1q_hbm")).ap()
    w2q_hbm = nc.dram_tensor("w2q_hbm", [D, 8192], BF16, kind=dk("w2q_hbm")).ap()
    dbg_pt = None
    if "dbg_pt" in debug:
        dbg_pt = nc.dram_tensor("dbg_pt", [128, 4608], BF16, kind="ExternalOutput").ap()

    wqkv_r = wqkv.rearrange("(dc p) n -> p dc n", p=128)
    wout_r = wout.rearrange("(cc p) n -> p cc n", p=128)

    with contextlib.ExitStack() as stk:
        def sbt(name, shape, dt):
            return stk.enter_context(nc.sbuf_tensor(name, shape, dt))

        A32t = sbt("A32", [128, 16384], F32)
        A32 = A32t.ap()
        A16 = A32.bitcast(BF16)
        B16 = sbt("B16", [128, 33024], BF16).ap()
        C32 = sbt("C32", [128, 12288], F32).ap()
        C16 = C32.bitcast(BF16)
        p1buf = sbt("p1buf", [128, 2048], F32).ap()
        p1buf2 = sbt("p1buf2", [128, 1024], F32).ap()
        evq = sbt("evq", [128, 2048], BF16).ap()
        cst = sbt("cst", [128, 8], F32).ap()
        onesb = sbt("onesb", [128, 128], BF16).ap()
        ones32r = sbt("ones32r", [1, 128], F32).ap()
        bqcols = sbt("bqcols", [128, 32], F32).ap()
        qbias = sbt("qbias", [128, 16], F32).ap()
        b1cols = sbt("b1cols", [128, 64], F32).ap()
        brows = sbt("brows", [1, 6144], BF16).ap()
        statc = sbt("statc", [128, 64], F32).ap()
        srow = sbt("srow", [1, 64], F32).ap()
        pacc = sbt("pacc", [128, 256], F32).ap()

        psA = stk.enter_context(nc.psum_tensor("psA", [128, 2048], F32)).ap()
        psB = stk.enter_context(nc.psum_tensor("psB", [128, 2048], F32)).ap()

        X = Ctx()
        for cn in SEMS:
            X.sem[cn] = stk.enter_context(nc.semaphore(cn))

        xlnv = lambda dc: A16[:, dc * 2048:(dc + 1) * 2048]
        ao_tm = lambda tt: A16[:, tt * 2048:(tt + 1) * 2048]
        x2v = lambda tt: A32[:, tt * 2048:(tt + 1) * 2048]
        vslot = lambda h, s: B16[:, (h * NT + s) * 129:(h * NT + s) * 129 + 129]
        ao_fm = lambda cc: B16[:, cc * 1024:(cc + 1) * 1024]
        w1qTv = lambda dc: B16[:, 16384 + dc * 1024:16384 + (dc + 1) * 1024]
        w2qTv = lambda fc: B16[:, fc * 2048:(fc + 1) * 2048]
        xt = [C32[:, 0:2048], C32[:, 2048:4096]]
        sqf = [C32[:, 4096:6144], C32[:, 6144:8192]]
        lntm = [C16[:, 16384:18432], C16[:, 18432:20480]]
        qk32 = [C32[:, 0:4096], C32[:, 4096:8192]]
        qkbf = [C16[:, 16384:20480], C16[:, 20480:24576]]
        v32 = C32[:, 0:4096]
        vbf = [C16[:, 16384:20480], C16[:, 20480:24576]]
        kTb = [C16[:, 0:2048], C16[:, 2048:4096]]
        qTb = [C16[:, 4096:5120], C16[:, 5120:6144]]
        woutst = [C16[:, 0:8192], C16[:, 8192:16384]]
        x2lnfm = lambda dc: C16[:, dc * 1024:(dc + 1) * 1024]
        hgv = lambda fc: C16[:, 16384 + fc * 1024:16384 + (fc + 1) * 1024]
        ln2sq = C32[:, 4096:6144]
        p1s = [p1buf[:, 0:1024], p1buf[:, 1024:2048], p1buf2[:, 0:1024]]
        p1tv4 = [evq[:, 0:1024], evq[:, 1024:2048],
                 C16[:, 9216:10240], C16[:, 10240:11264]]
        NQIL = 192

        def ptb(u):
            return u % 4 if u < NQIL else u % 2

        def ptprev(u):
            return u - 4 if u < NQIL + 2 else u - 2
        PTv = [C16[:, 6144:7680], C16[:, 7680:9216]]
        msk = C16[:, 11264:15360]
        x2st = C16[:, 14336:16384]

        LN_S0, LN_S2, LN_MEAN, LN_E2, LN_VAR, LN_RSTD, LN_NMR = 0, 1, 2, 3, 4, 5, 6
        NEGC = 8
        T1C, NT1C, T2C, NT2C = 26, 27, 28, 29
        RECC = 30

        sched = [_chunks(KJ[j]) for j in range(NPAIR)]
        nchunk_pair = [len(s) for s in sched]
        gchunk_of = {}
        g = 0
        for h in range(H):
            for j in range(NPAIR):
                for ci in range(nchunk_pair[j]):
                    gchunk_of[(h, j, ci)] = g
                    g += 1
        NGCH = g

        def p0_sync(q):
            pass

        def p0_pool(gp):
            X.dma("iniP", gp.dma_start(out=bqcols, in_=bqkv[0:4096].rearrange("(m p) -> p m", p=128),
                                     allow_slow_non_contiguous=True))
            X.dma("iniP", gp.dma_start(out=b1cols, in_=b1.rearrange("(m p) -> p m", p=128),
                                     allow_slow_non_contiguous=True))
            X.dma("iniP", gp.dma_start(out=brows[0:1, 0:2048],
                                     in_=bqkv.rearrange("(a n) -> a n", a=3)[2:3, :]))
            X.dma("iniP", gp.dma_start(out=brows[0:1, 2048:4096],
                                     in_=bout.rearrange("(a n) -> a n", a=1)))
            X.dma("iniP", gp.dma_start(out=brows[0:1, 4096:6144],
                                     in_=b2.rearrange("(a n) -> a n", a=1)))
            X.mark("init_pool", "iniP")

        def p0_vec(v):
            v.memset(cst[:, 0:1], 1.0)
            v.memset(cst[:, 1:2], EPS)
            v.memset(cst[:, 2:3], QSCALE)
            v.memset(onesb, 1.0)
            v.memset(ones32r, 1.0)
            X.bump("cV", v.memset(B16[:, 128:H * NT * 129:129], 1.0))
            X.wait(v, "init_pool")
            X.bump("cV", v.tensor_scalar(out=qbias, in0=bqcols[:, 0:16], scalar1=QSCALE,
                                         scalar2=None, op0=ALU.mult))
            X.mark("init_vec", "cV")

        def p1_sync(q):
            for i in range(NT):
                if i >= 2:
                    X.wait(q, f"ln1_norm_{i-2}")
                    X.wait(q, f"ln1_r0_{i-2}")
                X.dma(f"lx{i % 2}", q.dma_start(out=xt[i % 2], in_=xg[i * 128:(i + 1) * 128, :]))
                X.mark(f"ln1_ld_{i}", f"lx{i % 2}")

        def p1_scalar(s):
            for i in range(NT):
                X.wait(s, f"ln1_ld_{i}")
                if i >= 2:
                    X.wait(s, f"ln1_sp_{i-2}")
                    X.wait(s, f"ln1_r2_{i-2}")
                X.bump("cS", s.square(out=sqf[i % 2], in_=xt[i % 2]))
                X.mark(f"ln1_sq_{i}", "cS")
                X.wait(s, f"ln1_var_{i}")
                X.chain(s, "cS", s.activation(out=statc[:, LN_RSTD:LN_RSTD + 1],
                                              in_=statc[:, LN_VAR:LN_VAR + 1], func=AF.Ln,
                                              bias=cst[:, 1:2]))
                X.chain(s, "cS", s.activation(out=statc[:, LN_RSTD:LN_RSTD + 1],
                                              in_=statc[:, LN_RSTD:LN_RSTD + 1], func=AF.Exp,
                                              scale=-0.5))
                X.mark(f"ln1_rstd_{i}", "cS")
                X.wait(s, f"ln1_nmr_{i}")
                X.chain(s, "cS", s.activation(out=lntm[i % 2], in_=xt[i % 2], func=AF.Identity,
                                              scale=statc[:, LN_RSTD:LN_RSTD + 1],
                                              bias=statc[:, LN_NMR:LN_NMR + 1]))
                X.mark(f"ln1_norm_{i}", "cS")
                X.dma(f"xs{i % 2}", s.dma_start(out=xln_hbm[i * 128:(i + 1) * 128, :], in_=lntm[i % 2]))
                X.mark(f"ln1_sp_{i}", f"xs{i % 2}")
            X.wait(s, "ln1_sp_14")
            X.wait(s, "ln1_sp_15")
            for dc in range(NT):
                X.dma("xlx", s.dma_start(out=xlnv(dc), in_=xln_hbm[:, dc * 128:(dc + 1) * 128],
                                        transpose=True))
            X.mark("xln_fm", "xlx")

        def p1_vec(v):
            for i in range(NT):
                X.wait(v, f"ln1_ld_{i}")
                if i >= 1:
                    X.wait(v, f"ln1_norm_{i-1}")
                X.chain(v, "cV", v.tensor_reduce(out=statc[:, LN_S0:LN_S0 + 1], in_=xt[i % 2],
                                                 axis=AX.X, op=ALU.add))
                X.mark(f"ln1_r0_{i}", "cV")
                X.wait(v, f"ln1_sq_{i}")
                X.chain(v, "cV", v.tensor_reduce(out=statc[:, LN_S2:LN_S2 + 1], in_=sqf[i % 2],
                                                 axis=AX.X, op=ALU.add))
                X.mark(f"ln1_r2_{i}", "cV")
                X.chain(v, "cV", v.tensor_scalar(out=statc[:, LN_MEAN:LN_MEAN + 1],
                                                 in0=statc[:, LN_S0:LN_S0 + 1], scalar1=INV_D,
                                                 scalar2=None, op0=ALU.mult))
                X.chain(v, "cV", v.tensor_scalar(out=statc[:, LN_E2:LN_E2 + 1],
                                                 in0=statc[:, LN_S2:LN_S2 + 1], scalar1=INV_D,
                                                 scalar2=None, op0=ALU.mult))
                X.chain(v, "cV", v.tensor_mul(out=statc[:, LN_VAR:LN_VAR + 1],
                                              in0=statc[:, LN_MEAN:LN_MEAN + 1],
                                              in1=statc[:, LN_MEAN:LN_MEAN + 1]))
                X.chain(v, "cV", v.tensor_sub(out=statc[:, LN_VAR:LN_VAR + 1],
                                              in0=statc[:, LN_E2:LN_E2 + 1],
                                              in1=statc[:, LN_VAR:LN_VAR + 1]))
                X.mark(f"ln1_var_{i}", "cV")
                X.wait(v, f"ln1_rstd_{i}")
                X.chain(v, "cV", v.tensor_scalar(out=statc[:, LN_NMR:LN_NMR + 1],
                                                 in0=statc[:, LN_MEAN:LN_MEAN + 1],
                                                 scalar1=statc[:, LN_RSTD:LN_RSTD + 1],
                                                 scalar2=-1.0, op0=ALU.mult, op1=ALU.mult))
                X.mark(f"ln1_nmr_{i}", "cV")
            X.mark("ln1_vdone", "cV")

        def p1src(u):
            if u < NQU:
                r, cp = u // 2, u % 2
                return w1[r * 128:(r + 1) * 128, cp * 1024:(cp + 1) * 1024]
            r, cp = (u - NQU) // 8, (u - NQU) % 8
            return w2[r * 128:(r + 1) * 128, cp * 1024:(cp + 1) * 1024]

        def pqdst(u):
            if u < NQU:
                r, cp = u // 2, u % 2
                return w1q_hbm[r * 128:(r + 1) * 128, cp * 1024:(cp + 1) * 1024]
            r, cp = (u - NQU) // 8, (u - NQU) % 8
            return w2q_hbm[r * 128:(r + 1) * 128, cp * 1024:(cp + 1) * 1024]

        def p2_sync(q):
            X.wait(q, "ln1_norm_15")
            X.wait(q, "ln1_vdone")
            for r in range(16):
                if r >= 2:
                    X.wait(q, f"qk_conv_{r-2}")
                c0 = r * 256 if r < 8 else 2048 + (r - 8) * 256
                for d4 in range(4):
                    X.dma(f"qw{r % 2}", q.dma_start(
                        out=qk32[r % 2][:, d4 * 1024:(d4 + 1) * 1024]
                            .rearrange("p (dc n) -> p dc n", n=256),
                        in_=wqkv_r[:, d4 * 4:(d4 + 1) * 4, c0:c0 + 256]))
                X.mark(f"qk_ld_{r}", f"qw{r % 2}")
                for k in range(16):
                    u = r * 16 + k
                    if u >= 3:
                        X.wait(q, f"p1red_{u-3}")
                    X.dma(f"lq{u % 3}", q.dma_start(out=p1s[u % 3], in_=p1src(ORD[u])))
                    X.mark(f"p1ld_{u}", f"lq{u % 3}")
            for vb in range(8):
                if vb == 0:
                    X.wait(q, "qk_conv_14")
                if vb >= 1:
                    X.wait(q, f"v_conv_{vb-1}")
                c0 = 4096 + vb * 256
                for d4 in range(4):
                    X.dma("vw", q.dma_start(
                        out=v32[:, d4 * 1024:(d4 + 1) * 1024]
                            .rearrange("p (dc n) -> p dc n", n=256),
                        in_=wqkv_r[:, d4 * 4:(d4 + 1) * 4, c0:c0 + 256]))
                X.mark(f"v_ld_{vb}", "vw")

        def p2_tensor(t):
            X.wait(t, "xln_fm")
            if True:
                for m in range(32):
                    X.wait(t, f"qk_conv_{m // 2}")
                    if m >= 2:
                        X.wait(t, f"qk_evd_{m-2}")
                    ps = psA if m % 2 == 0 else psB
                    inst = None
                    ntt = 2 if m < 16 else 4
                    for dc in range(16):
                        lhs = qkbf[(m // 2) % 2][:, dc * 256 + (m % 2) * 128:
                                                 dc * 256 + (m % 2) * 128 + 128]
                        for t4 in range(ntt):
                            inst = t.matmul(ps[:, t4 * 512:(t4 + 1) * 512], lhs,
                                            xlnv(dc)[:, t4 * 512:(t4 + 1) * 512],
                                            start=(dc == 0), stop=(dc == 15))
                    X.bump("cT", inst)
                    X.mark(f"qk_mm_{m}", "cT")
            X.wait(t, "init_pool")
            for vb in range(8):
                X.wait(t, f"v_conv_{vb}")
                for tt in range(16):
                    u = vb * 16 + tt
                    bank = tt % 8
                    ps = psA if bank < 4 else psB
                    po = ps[:, (bank % 4) * 512:(bank % 4) * 512 + 256]
                    if u == 0:
                        X.wait(t, "qk_evd_30")
                    if u == 4:
                        X.wait(t, "qk_evd_31")
                    if u >= 8:
                        X.wait(t, f"v_ev_{u-8}")
                    for dc in range(16):
                        t.matmul(po, xlnv(dc)[:, tt * 128:(tt + 1) * 128],
                                 vbf[vb % 2][:, dc * 256:(dc + 1) * 256],
                                 start=(dc == 0), stop=False)
                    inst = t.matmul(po, onesb[0:1, :], brows[0:1, vb * 256:vb * 256 + 256],
                                    start=False, stop=True)
                    X.bump("cT", inst)
                    X.mark(f"v_mm_{u}", "cT")
            X.mark("qkv_done", "cT")

        def p2_vec(v):
            for r in range(16):
                X.wait(v, f"qk_ld_{r}")
                if r == 0:
                    X.wait(v, "ln1_sp_14")
                    X.wait(v, "ln1_sp_15")
                if r >= 2:
                    X.wait(v, f"qk_mm_{2 * r - 3}")
                X.bump("cV", v.tensor_copy(out=qkbf[r % 2], in_=qk32[r % 2]))
                X.mark(f"qk_conv_{r}", "cV")
        def conv_vb(v, vb):
            X.wait(v, f"v_ld_{vb}")
            if vb == 0:
                X.wait(v, "qk_mm_31")
            if vb >= 2:
                X.wait(v, f"v_mm_{(vb-2)*16+15}")
            X.bump("cV", v.tensor_copy(out=vbf[vb % 2], in_=v32))
            X.mark(f"v_conv_{vb}", "cV")

        def p2_vec_b(v):
            conv_vb(v, 0)
            for vb in range(8):
                if vb + 1 < 8:
                    conv_vb(v, vb + 1)
                for tt in range(16):
                    u = vb * 16 + tt
                    X.wait(v, f"v_mm_{u}")
                    bank = tt % 8
                    ps = psA if bank < 4 else psB
                    po = ps[:, (bank % 4) * 512:(bank % 4) * 512 + 256]
                    X.bump("cV", v.tensor_copy(out=vslot(2 * vb, tt)[:, 0:128], in_=po[:, 0:128]))
                    X.bump("cV", v.tensor_copy(out=vslot(2 * vb + 1, tt)[:, 0:128],
                                               in_=po[:, 128:256]))
                    X.mark(f"v_ev_{u}", "cV")
            X.mark("v_all", "cV")

        def p2_scalar(s):
            stp = {"u": 0}

            def p1red(k):
                for _ in range(k):
                    u = stp["u"]
                    if u >= 2 * NQU:
                        return
                    X.wait(s, f"p1ld_{u}")
                    lu = ORD[u]
                    X.bump("cS", s.activation(out=p1s[u % 3], in_=p1s[u % 3], func=AF.Abs,
                                              accum_out=pacc[:, lu:lu + 1]))
                    X.mark(f"p1red_{u}", "cS")
                    stp["u"] += 1
            for m in range(32):
                p1red(8)
                X.wait(s, f"qk_mm_{m}")
                if m >= 1:
                    X.wait(s, f"qk_sp_{m-1}")
                ps = psA if m % 2 == 0 else psB
                if m < 16:
                    for t4 in range(2):
                        X.bump("cS", s.activation(out=evq[:, t4 * 512:(t4 + 1) * 512],
                                                  in_=ps[:, t4 * 512:(t4 + 1) * 512],
                                                  func=AF.Identity, scale=cst[:, 2:3],
                                                  bias=qbias[:, m:m + 1]))
                    X.mark(f"qk_evd_{m}", "cS")
                    X.wait(s, f"qk_evd_{m}")
                    X.dma("qsp", s.dma_start(out=q_hbm[m * 128:(m + 1) * 128, :],
                                            in_=evq[:, 0:1024]))
                else:
                    for t4 in range(4):
                        X.bump("cS", s.activation(out=evq[:, t4 * 512:(t4 + 1) * 512],
                                                  in_=ps[:, t4 * 512:(t4 + 1) * 512],
                                                  func=AF.Identity, scale=cst[:, 0:1],
                                                  bias=bqcols[:, m:m + 1]))
                    X.mark(f"qk_evd_{m}", "cS")
                    X.wait(s, f"qk_evd_{m}")
                    X.dma("qsp", s.dma_start(out=k_hbm[(m - 16) * 128:(m - 15) * 128, :],
                                            in_=evq[:, 0:2048]))
                X.mark(f"qk_sp_{m}", "qsp")
            p1red(2 * NQU)
            X.mark("p1_all", "cS")
            X.mark("qkv_spills", "qsp")


        def p2c_sync(q):
            X.wait(q, "qkv_spills")
            X.wait(q, "v_conv_7")
            for h in range(H):
                if h >= 2:
                    X.wait(q, f"bnred_{h-2}")
                X.dma(f"kq{h % 2}", q.dma_start(out=kTb[h % 2], in_=k_hbm[h * 128:(h + 1) * 128, :]))
                X.dma(f"kq{h % 2}", q.dma_start(out=qTb[h % 2], in_=q_hbm[h * 128:(h + 1) * 128, :]))
                X.mark(f"bn_ld_{h}", f"kq{h % 2}")

        def p2c_scalar(s):
            X.wait(s, "qkv_spills")
            for h in range(H):
                X.wait(s, f"bn_ld_{h}")
                if h >= 1:
                    X.wait(s, f"bnq_{h-1}")
                X.bump("cS", s.square(out=evq[:, 0:2048], in_=kTb[h % 2]))
                X.mark(f"bnsqk_{h}", "cS")
                X.wait(s, f"bnk_{h}")
                X.bump("cS", s.square(out=evq[:, 0:1024], in_=qTb[h % 2]))
                X.mark(f"bnsqq_{h}", "cS")
            X.wait(s, "bnred_all")
            X.chain(s, "cS", s.activation(out=srow[0:1, 32:64], in_=srow[0:1, 32:64], func=AF.Ln))
            X.mark("bnln", "cS")
            X.wait(s, "bnadd")
            X.chain(s, "cS", s.activation(out=srow[0:1, 32:48], in_=srow[0:1, 32:48], func=AF.Exp,
                                          scale=0.5))
            X.mark("bnexp", "cS")

        def p2c_tensor(t):
            X.wait(t, "v_all")
            for h in range(H):
                X.wait(t, f"bnsqk_{h}")
                if h >= 1:
                    X.wait(t, f"bnred_{h-1}")
                inst = None
                for p_ in range(4):
                    inst = t.matmul(psA[0:1, p_ * 512:(p_ + 1) * 512], onesb[:, 0:1],
                                    evq[:, p_ * 512:(p_ + 1) * 512], start=True, stop=True)
                X.bump("cT", inst)
                X.mark(f"bnk_{h}", "cT")
                X.wait(t, f"bnsqq_{h}")
                inst = t.matmul(psB[0:1, 0:512], onesb[:, 0:1], evq[:, 0:512],
                                start=True, stop=True)
                inst = t.matmul(psB[0:1, 512:1024], onesb[:, 0:1], evq[:, 512:1024],
                                start=True, stop=True)
                X.bump("cT", inst)
                X.mark(f"bnq_{h}", "cT")
            X.wait(t, "bnneg")
            X.bump("cT", t.matmul(psB[:, 0:16], ones32r, srow[0:1, 32:48],
                                  start=True, stop=True))
            X.mark("bnt", "cT")

        def p2c_vec(v):
            for h in range(H):
                X.wait(v, f"bnk_{h}")
                X.chain(v, "cV", v.reduce_max(out=srow[0:1, 32 + h:33 + h],
                                              in_=psA[0:1, 0:2048], axis=AX.X))
                X.wait(v, f"bnq_{h}")
                X.chain(v, "cV", v.reduce_max(out=srow[0:1, 48 + h:49 + h],
                                              in_=psB[0:1, 0:1024], axis=AX.X))
                X.mark(f"bnred_{h}", "cV")
            X.mark("bnred_all", "cV")
            X.wait(v, "bnln")
            X.chain(v, "cV", v.tensor_add(out=srow[0:1, 32:48], in0=srow[0:1, 32:48],
                                          in1=srow[0:1, 48:64]))
            X.mark("bnadd", "cV")
            X.wait(v, "bnexp")
            X.chain(v, "cV", v.tensor_scalar(out=srow[0:1, 32:48], in0=srow[0:1, 32:48],
                                             scalar1=-1.0, scalar2=-1.0,
                                             op0=ALU.mult, op1=ALU.add))
            X.mark("bnneg", "cV")
            X.wait(v, "bnt")
            X.chain(v, "cV", v.tensor_copy(out=statc[:, 8:24], in_=psB[:, 0:16]))
            X.mark("negc_all", "cV")

        QS = {"ld": 0, "sp": 0, "q": 0, "g": 0}

        def qld_issue(q, k):
            for _ in range(k):
                u = QS["ld"]
                if u >= 2 * NQU:
                    return
                if u < 3:
                    X.wait(q, "p1_all")
                else:
                    X.wait(q, f"wqf_{u-3}")
                X.dma(f"lq{u % 3}", q.dma_start(out=p1s[u % 3], in_=p1src(ORD[u])))
                X.mark(f"wqld_{u}", f"lq{u % 3}")
                QS["ld"] += 1

        def qsp_issue(q, k):
            for _ in range(k):
                u = QS["sp"]
                if u >= 2 * NQU:
                    return
                X.wait(q, f"wqf_{u}")
                X.dma(f"ws{ptb(u)}", q.dma_start(out=pqdst(ORD[u]), in_=p1tv4[ptb(u)]))
                X.mark(f"wqsp_{u}", f"ws{ptb(u)}")
                if (u + 1) % 32 == 0:
                    g_ = (u + 1) // 32 - 1
                    for sw in range(4):
                        X.mark(f"qgrp_{g_}_{sw}", f"ws{sw}")
                QS["sp"] += 1

        def quant(v, k):
            for _ in range(k):
                u = QS["q"]
                if u >= 2 * NQU:
                    return
                if u == 0:
                    X.wait(v, "qkv_spills")
                    X.wait(v, "bnq_15")
                X.wait(v, f"wqld_{u}")
                pu = ptprev(u)
                if pu >= 0:
                    X.wait(v, f"wqsp_{pu}")
                lu = ORD[u]
                tc, ntc = (T1C, NT1C) if lu < NQU else (T2C, NT2C)
                X.chain(v, "cV", v.tensor_scalar(out=p1tv4[ptb(u)], in0=p1s[u % 3],
                                                 scalar1=statc[:, ntc:ntc + 1],
                                                 scalar2=None, op0=ALU.is_le))
                X.bump("cV", v.scalar_tensor_tensor(out=p1tv4[ptb(u)], in0=p1s[u % 3],
                                                    scalar=statc[:, tc:tc + 1],
                                                    in1=p1tv4[ptb(u)],
                                                    op0=ALU.is_ge, op1=ALU.subtract))
                X.mark(f"wqf_{u}", "cV")
                QS["q"] += 1

        def p3_sync(q):
            X.wait(q, "qkv_spills")
            X.wait(q, "v_all")
            X.wait(q, "bnexp")
            X.dma("mskq", q.dma_start(out=msk, in_=masks))
            X.mark("msk_ld", "mskq")
            def kqld(h):
                if h >= 2:
                    X.wait(q, f"head_done_{h-2}")
                X.dma(f"kq{h % 2}", q.dma_start(out=kTb[h % 2][0:64, :],
                                                in_=k_hbm[h * 128:h * 128 + 64, :]))
                X.dma(f"kq{h % 2}", q.dma_start(out=kTb[h % 2][64:128, :],
                                                in_=k_hbm[h * 128 + 64:(h + 1) * 128, :]))
                X.dma(f"kq{h % 2}", q.dma_start(out=qTb[h % 2], in_=q_hbm[h * 128:(h + 1) * 128, :]))
                X.mark(f"kq_ld_{h}", f"kq{h % 2}")
            kqld(0)
            kqld(1)
            for h in range(H):
                if h + 2 < H:
                    kqld(h + 2)
                qld_issue(q, 12)

        def p3_pool(gp):
            qsp_issue(gp, NQIL)

        def p3_tensor(t):
            def thr_w1(t):
                X.wait(t, "thrv_w1_a")
                inst = t.matmul(psA[0:1, 0:1], cst[:, 0:1], statc[:, 32:33],
                                start=True, stop=True)
                inst = t.matmul(psA[0:1, 1:2], cst[:, 0:1], statc[:, 33:34],
                                start=True, stop=True)
                X.bump("cT", inst)
                X.mark("thrt_w1_a", "cT")
                X.wait(t, "thrv_w1_b")
                inst = t.matmul(psA[:, 0:1], ones32r, srow[0:1, 18:19], start=True, stop=True)
                inst = t.matmul(psA[:, 1:2], ones32r, srow[0:1, 19:20], start=True, stop=True)
                X.bump("cT", inst)
                X.mark("thrt_w1_b", "cT")
                X.wait(t, "thrcp_w1")

            def thr_w2(t):
                X.wait(t, "thrv_w2_a")
                inst = t.matmul(psA[0:1, 0:1], cst[:, 0:1], statc[:, 34:35],
                                start=True, stop=True)
                inst = t.matmul(psA[0:1, 1:2], cst[:, 0:1], statc[:, 35:36],
                                start=True, stop=True)
                X.bump("cT", inst)
                X.mark("thrt_w2_a", "cT")
                X.wait(t, "thrv_w2_b")
                inst = t.matmul(psA[:, 0:1], ones32r, srow[0:1, 18:19], start=True, stop=True)
                inst = t.matmul(psA[:, 1:2], ones32r, srow[0:1, 19:20], start=True, stop=True)
                X.bump("cT", inst)
                X.mark("thrt_w2_b", "cT")
                X.wait(t, "thrcp_w2")

            def emit_sc(t, it):
                h, j, ci, c0, n, K, kl = it
                gch = gchunk_of[(h, j, ci)]
                par = gch % 2
                psS = psA if par == 0 else psB
                if j == 0 and ci == 0:
                    X.wait(t, f"kq_ld_{h}")
                if gch >= 2:
                    X.wait(t, f"exp_{gch-2}")
                inst = None
                for u in range(n):
                    s_ = kl[c0 + u]
                    inst = t.matmul(psS[:, u * 256:(u + 1) * 256],
                                    kTb[h % 2][:, s_ * 128:(s_ + 1) * 128],
                                    qTb[h % 2][:, j * 256:(j + 1) * 256],
                                    start=True, stop=True)
                X.bump("cT", inst)
                X.mark(f"sc_{gch}", "cT")

            def emit_av(t, it):
                h, j, ci, c0, n, K, kl = it
                gch = gchunk_of[(h, j, ci)]
                par = gch % 2
                X.wait(t, f"exp_{gch}")
                if (h, j, ci) in mask_ev:
                    X.wait(t, mask_ev[(h, j, ci)])
                if ci == 0:
                    X.wait(t, f"aoev_{h}_{j-1}" if j > 0 else
                           (f"aoev_{h-1}_{NPAIR-1}" if h > 0 else "negc_all"))
                inst = None
                for u in range(n):
                    pos = c0 + u
                    s_ = kl[pos]
                    for qt in range(2):
                        pav = psA if qt == 0 else psB
                        inst = t.matmul(pav[:, 1536:1665],
                                        PTv[par][:, u * 256 + qt * 128:u * 256 + qt * 128 + 128],
                                        vslot(h, s_),
                                        start=(pos == 0),
                                        stop=(pos == K - 1))
                X.bump("cT", inst)
                X.mark(f"av_{gch}", "cT")
                if j == NPAIR - 1 and ci == nchunk_pair[j] - 1:
                    X.mark(f"head_done_{h}", "cT")

            X.wait(t, "negc_all")
            thr_w1(t)
            thr_w2(t)
            SKEW = 2
            seq = []
            for h in range(H):
                for j in range(NPAIR):
                    K = KJ[j]
                    kl = KLIST[K]
                    for ci, (c0, n) in enumerate(sched[j]):
                        seq.append((h, j, ci, c0, n, K, kl))
            for idx, it in enumerate(seq):
                emit_sc(t, it)
                if idx >= SKEW:
                    emit_av(t, seq[idx - SKEW])
            for it in seq[-SKEW:]:
                emit_av(t, it)
            X.mark("attn_done", "cT")

        def p3_scalar(s):
            for h in range(H):
                for j in range(NPAIR):
                    for ci, (c0, n) in enumerate(sched[j]):
                        gch = gchunk_of[(h, j, ci)]
                        par = gch % 2
                        psS = psA if par == 0 else psB
                        X.wait(s, f"sc_{gch}")
                        if gch >= 2:
                            X.wait(s, f"av_{gch-2}")
                        X.bump("cS", s.activation(out=PTv[par][:, 0:n * 256],
                                                  in_=psS[:, 0:n * 256], func=AF.Exp,
                                                  bias=statc[:, 8 + h:9 + h]))
                        X.mark(f"exp_{gch}", "cS")
            X.mark("attn_s_done", "cS")

        mask_ev = {}

        srowI = srow.bitcast(I32)

        def emit_thr_dekker(v):
            sh = psA[0:1, 0:1]
            sl = psA[0:1, 1:2]
            c = lambda i: srow[0:1, i:i + 1]
            ci = lambda i: srowI[0:1, i:i + 1]
            ch = X.chain
            ch(v, "cV", v.tensor_scalar(out=c(8), in0=sh, scalar1=2.0 ** -12, scalar2=SPL_C,
                                        op0=ALU.mult, op1=ALU.add))
            ch(v, "cV", v.tensor_scalar(out=c(8), in0=c(8), scalar1=-SPL_C, scalar2=4096.0,
                                        op0=ALU.add, op1=ALU.mult))
            ch(v, "cV", v.scalar_tensor_tensor(out=c(10), in0=c(8), scalar=-1.0, in1=sh,
                                               op0=ALU.mult, op1=ALU.add))
            ch(v, "cV", v.tensor_scalar(out=c(11), in0=sh, scalar1=CH, scalar2=None,
                                        op0=ALU.mult))
            ch(v, "cV", v.tensor_scalar(out=c(15), in0=c(8), scalar1=CH_A, scalar2=None,
                                        op0=ALU.mult))
            ch(v, "cV", v.tensor_sub(out=c(12), in0=c(15), in1=c(11)))
            for src, k in ((8, CH_B), (10, CH_A), (10, CH_B)):
                ch(v, "cV", v.tensor_scalar(out=c(15), in0=c(src), scalar1=k, scalar2=None,
                                            op0=ALU.mult))
                ch(v, "cV", v.tensor_add(out=c(12), in0=c(12), in1=c(15)))
            ch(v, "cV", v.tensor_scalar(out=c(15), in0=sh, scalar1=CL, scalar2=None,
                                        op0=ALU.mult))
            ch(v, "cV", v.tensor_add(out=c(12), in0=c(12), in1=c(15)))
            ch(v, "cV", v.tensor_scalar(out=c(15), in0=sl, scalar1=CH, scalar2=None,
                                        op0=ALU.mult))
            ch(v, "cV", v.tensor_add(out=c(12), in0=c(12), in1=c(15)))
            ch(v, "cV", v.tensor_add(out=c(13), in0=c(11), in1=c(12)))
            ch(v, "cV", v.tensor_sub(out=c(15), in0=c(13), in1=c(11)))
            ch(v, "cV", v.tensor_sub(out=c(14), in0=c(12), in1=c(15)))
            ch(v, "cV", v.tensor_scalar(out=c(15), in0=c(14), scalar1=0.0, scalar2=None,
                                        op0=ALU.is_gt))
            ch(v, "cV", v.tensor_scalar(out=c(16), in0=c(13), scalar1=ULP_PUSH, scalar2=None,
                                        op0=ALU.mult))
            ch(v, "cV", v.tensor_mul(out=c(16), in0=c(16), in1=c(15)))
            ch(v, "cV", v.tensor_add(out=c(18), in0=c(13), in1=c(16)))
            ch(v, "cV", v.tensor_scalar(out=c(19), in0=c(18), scalar1=-1.0, scalar2=None,
                                        op0=ALU.mult))

        def p3_vec(v):
            X.wait(v, "p1_all")
            for lv in [64, 32, 16, 8, 4, 2, 1]:
                X.chain(v, "cV", v.tensor_add(out=pacc[:, 0:lv], in0=pacc[:, 0:lv],
                                              in1=pacc[:, lv:2 * lv]))
            X.chain(v, "cV", v.tensor_scalar(out=statc[:, 36:37], in0=pacc[:, 0:1],
                                             scalar1=RND_C, scalar2=None, op0=ALU.add))
            X.chain(v, "cV", v.tensor_scalar(out=statc[:, 32:33], in0=statc[:, 36:37],
                                             scalar1=-RND_C, scalar2=None, op0=ALU.add))
            X.chain(v, "cV", v.tensor_sub(out=statc[:, 33:34], in0=pacc[:, 0:1],
                                          in1=statc[:, 32:33]))
            X.mark("thrv_w1_a", "cV")
            X.wait(v, "thrt_w1_a")
            emit_thr_dekker(v)
            X.mark("thrv_w1_b", "cV")
            X.wait(v, "thrt_w1_b")
            X.chain(v, "cV", v.tensor_copy(out=statc[:, T1C:T1C + 1], in_=psA[:, 0:1]))
            X.chain(v, "cV", v.tensor_copy(out=statc[:, NT1C:NT1C + 1], in_=psA[:, 1:2]))
            X.mark("thrcp_w1", "cV")
            for lv in [64, 32, 16, 8, 4, 2, 1]:
                X.chain(v, "cV", v.tensor_add(out=pacc[:, 128:128 + lv],
                                              in0=pacc[:, 128:128 + lv],
                                              in1=pacc[:, 128 + lv:128 + 2 * lv]))
            X.chain(v, "cV", v.tensor_scalar(out=statc[:, 36:37], in0=pacc[:, 128:129],
                                             scalar1=RND_C, scalar2=None, op0=ALU.add))
            X.chain(v, "cV", v.tensor_scalar(out=statc[:, 34:35], in0=statc[:, 36:37],
                                             scalar1=-RND_C, scalar2=None, op0=ALU.add))
            X.chain(v, "cV", v.tensor_sub(out=statc[:, 35:36], in0=pacc[:, 128:129],
                                          in1=statc[:, 34:35]))
            X.mark("thrv_w2_a", "cV")
            X.wait(v, "thrt_w2_a")
            emit_thr_dekker(v)
            X.mark("thrv_w2_b", "cV")
            X.wait(v, "thrt_w2_b")
            X.chain(v, "cV", v.tensor_copy(out=statc[:, T2C:T2C + 1], in_=psA[:, 0:1]))
            X.chain(v, "cV", v.tensor_copy(out=statc[:, NT2C:NT2C + 1], in_=psA[:, 1:2]))
            X.mark("thrcp_w2", "cV")
            for h in range(H):
                for j in range(NPAIR):
                    K = KJ[j]
                    kl_j = KLIST[K]
                    mpos = [kl_j.index(L) for L in MLOC[j]]
                    for ci, (c0, n) in enumerate(sched[j]):
                        runs = []
                        for si, p_ in enumerate(mpos):
                            if c0 <= p_ < c0 + n:
                                if runs and runs[-1][1] == p_ and runs[-1][3] == si:
                                    runs[-1] = (runs[-1][0], p_ + 1, runs[-1][2], si + 1)
                                else:
                                    runs.append((p_, p_ + 1, si, si + 1))
                        if not runs:
                            continue
                        gch = gchunk_of[(h, j, ci)]
                        par = gch % 2
                        if h == 0:
                            X.wait(v, "msk_ld")
                        X.wait(v, f"exp_{gch}")
                        for (p0, p1, s0, s1) in runs:
                            X.bump("cV", v.tensor_mul(
                                out=PTv[par][:, (p0 - c0) * 256:(p1 - c0) * 256],
                                in0=PTv[par][:, (p0 - c0) * 256:(p1 - c0) * 256],
                                in1=msk[:, (j * 4 + s0) * 256:(j * 4 + s1) * 256]))
                        ev = f"mskd_{h}_{j}_{ci}"
                        X.mark(ev, "cV")
                        if X.dry:
                            mask_ev[(h, j, ci)] = ev
                    lastg = gchunk_of[(h, j, nchunk_pair[j] - 1)]
                    X.wait(v, f"av_{lastg}")
                    X.chain(v, "cV", v.reciprocal(out=statc[:, RECC:RECC + 1], in_=psA[:, 1664:1665]))
                    X.bump("cV", v.tensor_scalar(out=ao_tm(2 * j)[:, h * 128:(h + 1) * 128],
                                                 in0=psA[:, 1536:1664],
                                                 scalar1=statc[:, RECC:RECC + 1], scalar2=None,
                                                 op0=ALU.mult))
                    X.chain(v, "cV", v.reciprocal(out=statc[:, RECC + 1:RECC + 2], in_=psB[:, 1664:1665]))
                    X.bump("cV", v.tensor_scalar(out=ao_tm(2 * j + 1)[:, h * 128:(h + 1) * 128],
                                                 in0=psB[:, 1536:1664],
                                                 scalar1=statc[:, RECC + 1:RECC + 2], scalar2=None,
                                                 op0=ALU.mult))
                    X.mark(f"aoev_{h}_{j}", "cV")
                quant(v, 12)
            X.mark("attn_v_done", "cV")

        def p4_scalar(s):
            X.wait(s, "attn_v_done")
            for t8 in range(8):
                X.dma("aoq", s.dma_start(out=ao_hbm[t8 * 128:(t8 + 1) * 128, :],
                                        in_=A16[:, t8 * 2048:(t8 + 1) * 2048]))
            X.mark("ao_spill", "aoq")
            X.wait(s, "ao_spill")
            X.wait(s, "attn_done")
            for cc in range(16):
                X.dma("aoq", s.dma_start(out=ao_fm(cc), in_=ao_hbm[:, cc * 128:(cc + 1) * 128],
                                        transpose=True))
            X.mark("ao_fm", "aoq")
            for sw in range(4):
                X.wait(s, f"qgrp_0_{sw}")
            for dc in range(16):
                X.dma("w1x", s.dma_start(
                    out=w1qTv(dc),
                    in_=w1q_hbm[0:GF, dc * 128:(dc + 1) * 128],
                    transpose=True))
            X.mark("w1x_0", "w1x")

        def p4_sync(q):
            X.wait(q, "ao_spill")
            for k4 in range(4):
                X.dma("x2q", q.dma_start(
                    out=A32[:, k4 * 4096:(k4 + 1) * 4096].rearrange("p (tt d) -> p tt d", d=2048),
                    in_=xg[k4 * 256:(k4 + 1) * 256, :].rearrange("(tt p) d -> p tt d", p=128)))
            X.mark("x2_init", "x2q")
            while QS["ld"] < 2 * NQU:
                qld_issue(q, 4)
            for b_ in range(3):
                X.mark(f"lqfin_{b_}", f"lq{b_}")

        def p4_pool(gp):
            X.wait(gp, "attn_done")
            X.wait(gp, f"wqsp_{NQIL - 2}")
            X.wait(gp, f"wqsp_{NQIL - 1}")
            for db in range(4):
                if db >= 2:
                    X.wait(gp, f"op_mm_{db-2}_7")
                for cc in range(16):
                    X.dma(f"wo{db % 2}", gp.dma_start(out=woutst[db % 2][:, cc * 512:(cc + 1) * 512],
                                             in_=wout_r[:, cc, db * 512:(db + 1) * 512]))
                X.mark(f"wout_{db}", f"wo{db % 2}")
                if db >= 1:
                    qsp_issue(gp, 16)
            qsp_issue(gp, 2 * NQU)
            for sw in range(4):
                X.mark(f"wsfin_{sw}", f"ws{sw}")

        def p4_tensor(t):
            def pov(tt):
                ps = psA if tt % 8 < 4 else psB
                return ps[:, (tt % 4) * 512:(tt % 4) * 512 + 512]
            X.wait(t, "attn_s_done")
            X.wait(t, "attn_v_done")
            X.wait(t, "ao_fm")
            for db in range(4):
                X.wait(t, f"wout_{db}")
                if db >= 1:
                    for tt in range(8):
                        X.wait(t, f"op_ev_{(db - 1) * 8 + tt}")
                for cc in range(16):
                    for tt in range(8):
                        t.matmul(pov(tt), ao_fm(cc)[:, tt * 128:(tt + 1) * 128],
                                 woutst[db % 2][:, cc * 512:(cc + 1) * 512],
                                 start=(cc == 0), stop=False)
                X.wait(t, "init_pool")
                for tt in range(8):
                    inst = t.matmul(pov(tt), onesb[0:1, :],
                                    brows[0:1, 2048 + db * 512:2560 + db * 512],
                                    start=False, stop=True)
                    X.bump("cT", inst)
                    X.mark(f"op_mm_{db}_{tt}", "cT")
            X.mark("op_done", "cT")

        def p4_vec(v):
            X.wait(v, "x2_init")
            for db in range(4):
                for tt in range(8):
                    u = db * 8 + tt
                    bank = u % 8
                    ps = psA if bank < 4 else psB
                    po = ps[:, (bank % 4) * 512:(bank % 4) * 512 + 512]
                    X.wait(v, f"op_mm_{db}_{tt}")
                    X.bump("cV", v.tensor_add(out=x2v(tt)[:, db * 512:(db + 1) * 512],
                                              in0=po, in1=x2v(tt)[:, db * 512:(db + 1) * 512]))
                    X.mark(f"op_ev_{u}", "cV")
                    quant(v, 2)
            X.mark("op_v_done", "cV")

        def p5_scalar(s):
            X.wait(s, "op_done")
            for sw in range(4):
                X.wait(s, f"qgrp_0_{sw}")
            for fc in range(8):
                X.dma("w2x", s.dma_start(
                    out=w2qTv(fc),
                    in_=w2q_hbm[:, fc * 128:(fc + 1) * 128],
                    transpose=True))
            X.mark("w2x_0", "w2x")
            for i in range(8):
                X.wait(s, f"op_ev_{24+i}")
                if i >= 1:
                    X.wait(s, f"ln2_var_{i-1}")
                X.chain(s, "cS", s.activation(out=ln2sq, in_=x2v(i), func=AF.Identity,
                                              accum_out=statc[:, LN_S0:LN_S0 + 1]))
                X.bump("cS", s.activation(out=ln2sq, in_=x2v(i), func=AF.Square,
                                          accum_out=statc[:, LN_S2:LN_S2 + 1]))
                X.mark(f"ln2_sums_{i}", "cS")
                X.wait(s, f"ln2_var_{i}")
                X.chain(s, "cS", s.activation(out=statc[:, LN_RSTD:LN_RSTD + 1],
                                              in_=statc[:, LN_VAR:LN_VAR + 1], func=AF.Ln,
                                              bias=cst[:, 1:2]))
                X.chain(s, "cS", s.activation(out=statc[:, LN_RSTD:LN_RSTD + 1],
                                              in_=statc[:, LN_RSTD:LN_RSTD + 1], func=AF.Exp,
                                              scale=-0.5))
                X.mark(f"ln2_rstd_{i}", "cS")
                X.wait(s, f"ln2_nmr_{i}")
                if i >= 1:
                    X.wait(s, f"ln2_sp_{i-1}")
                X.chain(s, "cS", s.activation(out=x2st,
                                              in_=x2v(i), func=AF.Identity,
                                              scale=statc[:, LN_RSTD:LN_RSTD + 1],
                                              bias=statc[:, LN_NMR:LN_NMR + 1]))
                X.mark(f"ln2_norm_{i}", "cS")
                X.dma("l2q", s.dma_start(out=x2ln_hbm[i * 128:(i + 1) * 128, :],
                                        in_=x2st))
                X.mark(f"ln2_sp_{i}", "l2q")
            X.mark("ln2_spills", "l2q")

        def p5_vec(v):
            for i in range(8):
                X.wait(v, f"ln2_sums_{i}")
                X.chain(v, "cV", v.tensor_scalar(out=statc[:, LN_MEAN:LN_MEAN + 1],
                                                 in0=statc[:, LN_S0:LN_S0 + 1], scalar1=INV_D,
                                                 scalar2=None, op0=ALU.mult))
                X.chain(v, "cV", v.tensor_scalar(out=statc[:, LN_E2:LN_E2 + 1],
                                                 in0=statc[:, LN_S2:LN_S2 + 1], scalar1=INV_D,
                                                 scalar2=None, op0=ALU.mult))
                X.chain(v, "cV", v.tensor_mul(out=statc[:, LN_VAR:LN_VAR + 1],
                                              in0=statc[:, LN_MEAN:LN_MEAN + 1],
                                              in1=statc[:, LN_MEAN:LN_MEAN + 1]))
                X.chain(v, "cV", v.tensor_sub(out=statc[:, LN_VAR:LN_VAR + 1],
                                              in0=statc[:, LN_E2:LN_E2 + 1],
                                              in1=statc[:, LN_VAR:LN_VAR + 1]))
                X.mark(f"ln2_var_{i}", "cV")
                X.wait(v, f"ln2_rstd_{i}")
                X.chain(v, "cV", v.tensor_scalar(out=statc[:, LN_NMR:LN_NMR + 1],
                                                 in0=statc[:, LN_MEAN:LN_MEAN + 1],
                                                 scalar1=statc[:, LN_RSTD:LN_RSTD + 1],
                                                 scalar2=-1.0, op0=ALU.mult, op1=ALU.mult))
                X.mark(f"ln2_nmr_{i}", "cV")

        def p6_scalar(s):
            for g in range(NGRP):
                for fc in range(8):
                    X.wait(s, f"f1_mm_{g}_{fc}")
                    if g > 0 and fc == 0:
                        X.wait(s, f"f2_mm_{g-1}_7")
                    for t2 in range(2):
                        bank = (2 * fc + t2) % 4
                        X.bump("cS", s.activation(
                            out=hgv(fc)[:, t2 * 512:(t2 + 1) * 512],
                            in_=psA[:, bank * 512:(bank + 1) * 512],
                            func=AF.Gelu, bias=b1cols[:, g * 8 + fc:g * 8 + fc + 1]))
                    X.mark(f"f1gelu_{g}_{fc}", "cS")

            for tb in range(8):
                X.wait(s, f"f2ev_{NGRP-1}_{tb}")
                X.dma("yq", s.dma_start(out=y[tb * 128:(tb + 1) * 128, :], in_=x2v(tb)))
            X.mark("y_stores", "yq")

        def p6_sync(q):
            X.wait(q, "ln2_spills")
            X.wait(q, "op_done")
            X.wait(q, "w1x_0")
            X.wait(q, "w2x_0")
            for dc in range(16):
                X.dma("l2qb", q.dma_start(out=x2lnfm(dc),
                                         in_=x2ln_hbm[:, dc * 128:(dc + 1) * 128],
                                         transpose=True))
            X.mark("x2ln_fm", "l2qb")
            for g in range(NGRP):
                if g >= 1:
                    X.wait(q, f"f2_mm_{g-1}_7")
                    for sw in range(4):
                        X.wait(q, f"qgrp_{g}_{sw}")
                    for fc in range(8):
                        X.dma("w2x", q.dma_start(
                            out=w2qTv(fc),
                            in_=w2q_hbm[:, g * GF + fc * 128:g * GF + (fc + 1) * 128],
                            transpose=True))
                    X.mark(f"w2x_{g}", "w2x")
                if g + 1 < NGRP:
                    X.wait(q, f"f1_mm_{g}_7")
                    for sw in range(4):
                        X.wait(q, f"qgrp_{g+1}_{sw}")
                    for dc in range(16):
                        X.dma("w1xb", q.dma_start(
                            out=w1qTv(dc),
                            in_=w1q_hbm[(g + 1) * GF:(g + 2) * GF, dc * 128:(dc + 1) * 128],
                            transpose=True))
                    X.mark(f"w1x_{g+1}", "w1xb")

        def p6_tensor(t):
            X.wait(t, "x2ln_fm")
            for g in range(NGRP):
                X.wait(t, f"w1x_{g}")
                for fc in range(8):
                    if g > 0 or fc >= 2:
                        pg, pf = (g, fc - 2) if fc >= 2 else (g - 1, fc + 6)
                        X.wait(t, f"f1gelu_{pg}_{pf}")
                    if g > 0 and fc < 2:
                        X.wait(t, f"f2ev_{g-1}_7")
                    inst = None
                    for dc in range(16):
                        lhs = w1qTv(dc)[:, fc * 128:(fc + 1) * 128]
                        for t2 in range(2):
                            bank = (2 * fc + t2) % 4
                            inst = t.matmul(psA[:, bank * 512:(bank + 1) * 512], lhs,
                                            x2lnfm(dc)[:, t2 * 512:(t2 + 1) * 512],
                                            start=(dc == 0), stop=(dc == 15))
                    X.bump("cT", inst)
                    X.mark(f"f1_mm_{g}_{fc}", "cT")
                X.wait(t, f"w2x_{g}")
                X.wait(t, f"f1gelu_{g}_7")
                for tb in range(8):
                    ps = psB if tb % 2 == 0 else psA
                    if g > 0 or tb >= 2:
                        pg, pt = (g, tb - 2) if tb >= 2 else (g - 1, tb + 6)
                        X.wait(t, f"f2ev_{pg}_{pt}")
                    inst = None
                    for fc in range(8):
                        lhs = hgv(fc)[:, tb * 128:(tb + 1) * 128]
                        for db in range(4):
                            inst = t.matmul(ps[:, db * 512:(db + 1) * 512], lhs,
                                            w2qTv(fc)[:, db * 512:(db + 1) * 512],
                                            start=(fc == 0),
                                            stop=(fc == 7 and g > 0))
                    if g == 0:
                        for db in range(4):
                            inst = t.matmul(ps[:, db * 512:(db + 1) * 512], onesb[0:1, :],
                                            brows[0:1, 4096 + db * 512:4608 + db * 512],
                                            start=False, stop=True)
                    X.bump("cT", inst)
                    X.mark(f"f2_mm_{g}_{tb}", "cT")
            X.mark("ffn_done", "cT")

        def p6_vec(v):
            for g in range(NGRP):
                for tb in range(8):
                    ps = psB if tb % 2 == 0 else psA
                    X.wait(v, f"f2_mm_{g}_{tb}")
                    for db in range(4):
                        X.bump("cV", v.tensor_add(out=x2v(tb)[:, db * 512:(db + 1) * 512],
                                                  in0=ps[:, db * 512:(db + 1) * 512],
                                                  in1=x2v(tb)[:, db * 512:(db + 1) * 512]))
                    X.mark(f"f2ev_{g}_{tb}", "cV")
            quant(v, 2 * NQU)
            X.mark("ffn_v_done", "cV")

        def p7_sync(q):
            X.wait(q, "y_stores")
            X.wait(q, "ffn_done")
            X.wait(q, "ffn_v_done")
            for b_ in range(3):
                X.wait(q, f"lqfin_{b_}")
            for sw in range(4):
                X.wait(q, f"wsfin_{sw}")

        PHASES = [
            (p0_sync, "sync"), (p0_pool, "gpsimd"), (p0_vec, "vector"),
            (p1_sync, "sync"), (p1_scalar, "scalar"), (p1_vec, "vector"),
            (p2_sync, "sync"), (p2_tensor, "tensor"),
            (p2_vec, "vector"), (p2_vec_b, "vector"), (p2_scalar, "scalar"),
            (p2c_sync, "sync"), (p2c_scalar, "scalar"), (p2c_tensor, "tensor"), (p2c_vec, "vector"),
            (p3_sync, "sync"), (p3_pool, "gpsimd"), (p3_tensor, "tensor"),
            (p3_scalar, "scalar"), (p3_vec, "vector"),
            (p4_scalar, "scalar"), (p4_sync, "sync"), (p4_pool, "gpsimd"),
            (p4_tensor, "tensor"), (p4_vec, "vector"),
            (p5_scalar, "scalar"), (p5_vec, "vector"),
            (p6_scalar, "scalar"), (p6_sync, "sync"),
            (p6_tensor, "tensor"), (p6_vec, "vector"),
            (p7_sync, "sync"),
        ]

        for fn, _ in PHASES:
            fn(_DryEng())
        X.reset()
        QS["ld"] = QS["sp"] = QS["q"] = QS["g"] = 0
        block = stk.enter_context(nc.Block())
        for fn, engname in PHASES:
            dec = getattr(block, engname)

            def _emit(eng, _fn=fn):
                _fn(eng)
            dec(_emit)

    return nc


_NC_CACHE = {}


def _perm(half):
    own = PAIRS_A if half == 0 else PAIRS_B
    par = PAIRS_B if half == 0 else PAIRS_A
    rows = []
    for ga, gb in own + par:
        rows.extend(range(ga * 128, ga * 128 + 128))
        rows.extend(range(gb * 128, gb * 128 + 128))
    return np.array(rows, dtype=np.int64)


def _build_masks(half):
    own = PAIRS_A if half == 0 else PAIRS_B
    par = PAIRS_B if half == 0 else PAIRS_A
    m = np.zeros((128, 4096), dtype=np.float32)
    p = np.arange(128)[:, None]
    qq = np.arange(128)[None, :]
    for j in range(NPAIR):
        for sidx in range(4):
            gk = own[j][sidx] if sidx < 2 else par[j][sidx - 2]
            kpos = gk * 128 + p
            base = (j * 4 + sidx) * 256
            m[:, base:base + 128] = (kpos <= own[j][0] * 128 + qq)
            m[:, base + 128:base + 256] = (kpos <= own[j][1] * 128 + qq)
    return m.astype(BF)


def kernel(L0_attn_x, L0_attn_w_qkv, L0_attn_b_qkv, L0_attn_w_out, L0_attn_b_out,
           L0_ffn_w1, L0_ffn_b1, L0_ffn_w2, L0_ffn_b2):
    x = np.ascontiguousarray(np.asarray(L0_attn_x, dtype=np.float32))
    wqkv = np.ascontiguousarray(np.asarray(L0_attn_w_qkv, dtype=np.float32))
    bqkv = np.ascontiguousarray(np.asarray(L0_attn_b_qkv, dtype=np.float32))
    wout = np.ascontiguousarray(np.asarray(L0_attn_w_out, dtype=np.float32))
    bout = np.ascontiguousarray(np.asarray(L0_attn_b_out, dtype=np.float32))
    w1 = np.ascontiguousarray(np.asarray(L0_ffn_w1, dtype=np.float32))
    b1 = np.ascontiguousarray(np.asarray(L0_ffn_b1, dtype=np.float32))
    w2 = np.ascontiguousarray(np.asarray(L0_ffn_w2, dtype=np.float32))
    b2 = np.ascontiguousarray(np.asarray(L0_ffn_b2, dtype=np.float32))
    B = x.shape[0]

    if "nc" not in _NC_CACHE:
        _NC_CACHE["nc"] = build_program()
    nc = _NC_CACHE["nc"]

    in_maps = []
    meta = []
    for c in range(8):
        b, half = c // 2, c % 2
        perm = _perm(half)
        meta.append((b, perm))
        in_maps.append({
            "xg": np.ascontiguousarray(x[b][perm]),
            "masks": _build_masks(half),
            "wqkv": wqkv, "bqkv": bqkv, "wout": wout, "bout": bout,
            "w1": w1, "b1": b1, "w2": w2, "b2": b2,
        })

    res = run_bass_kernel_spmd(nc, in_maps, core_ids=list(range(8)))
    out = np.empty((B, T, D), dtype=np.float32)
    for c in range(8):
        b, perm = meta[c]
        out[b][perm[:TO]] = res.results[c]["y"]
    return out


# version 09 (final)
# speedup vs baseline: 1.4824x; 1.0142x over previous
import contextlib
import numpy as np
import ml_dtypes
import concourse.bass as bass
import concourse.mybir as mybir
from concourse.bass_utils import run_bass_kernel_spmd

F32 = mybir.dt.float32
BF16 = mybir.dt.bfloat16
I32 = mybir.dt.int32
AF = mybir.ActivationFunctionType
ALU = mybir.AluOpType
AX = mybir.AxisListType
BF = ml_dtypes.bfloat16

SEMS = ["cT", "cS", "cV", "cG",
        "iniP", "lx0", "lx1", "xs0", "xs1", "xlx", "qw0", "qw1",
        "lq0", "lq1", "lq2", "lq3",
        "ws0", "ws1", "ws2", "ws3", "ws4", "ws5", "ws6", "ws7",
        "l2qb", "w1xb", "aoq2",
        "vw", "qsp", "mskq", "kq0", "kq1",
        "aoq", "x2q", "wo0", "wo1", "l2q", "w1x", "w2x", "yq"]

D = 2048
T = 2048
TO = 1024
H = 16
NT = 16
KJ = [16, 12, 8, 4]
NPAIR = 4
PAIRS_A = [(14, 15), (8, 9), (6, 7), (0, 1)]
PAIRS_B = [(12, 13), (10, 11), (4, 5), (2, 3)]
QSCALE = 1.0 / float(np.sqrt(128.0))
INV_D = 1.0 / D
EPS = 1e-5
NGRP = 8
GF = 1024
NQU = 128
ORD = []
for _g in range(NGRP):
    ORD += [16 * _g + _k for _k in range(16)]
    ORD += [NQU + _r * 8 + _g for _r in range(16)]
assert sorted(ORD) == list(range(2 * NQU))
_f = np.float32
_CHD = 0.7 / 16777216.0
CH = float(_f(_CHD))
CL = float(_f(_CHD - CH))
_c = _f(_f(CH) * _f(4097.0))
CH_A = float(_f(_c - _f(_c - _f(CH))))
CH_B = float(_f(_f(CH) - _f(CH_A)))
RND_C = 12582912.0
SPL_C = 12582912.0
ULP_PUSH = float(_f(2.0 ** -24 * 1.001))


KLIST = {
    16: list(range(16)),
    12: [2, 3, 4, 5, 6, 7, 10, 11, 12, 13, 14, 15],
    8: [4, 5, 6, 7, 12, 13, 14, 15],
    4: [6, 7, 14, 15],
}
MLOC = {j: [2 * j, 2 * j + 1, 8 + 2 * j, 9 + 2 * j] for j in range(4)}


def _chunks(n, c=6):
    out, i = [], 0
    while i < n:
        out.append((i, min(c, n - i)))
        i += c
    return out


class _DryInst:
    def then_inc(self, sem, k):
        return self


class _DryReg:
    def __enter__(self):
        return self

    def __exit__(self, *a):
        return False


class _DryEng:
    def register(self, *a, **k):
        return _DryReg()

    def snap(self, *a, **k):
        return None

    def wait_ge(self, *a, **k):
        return None

    def __getattr__(self, name):
        def f(*a, **k):
            return _DryInst()
        return f


class Ctx:
    def __init__(self):
        self.dry = True
        self.ctr = {n: 0 for n in SEMS}
        self.ev = {}
        self.sem = {}

    def reset(self):
        self.dry = False
        for k in self.ctr:
            self.ctr[k] = 0

    def bump(self, cname, inst, k=1):
        if not self.dry:
            inst.then_inc(self.sem[cname], k)
        self.ctr[cname] += k
        return self.ctr[cname]

    def dma(self, cname, inst):
        return self.bump(cname, inst, 16)

    def mark(self, name, cname):
        if self.dry:
            assert name not in self.ev, f"duplicate event {name}"
            self.ev[name] = (cname, self.ctr[cname])
        else:
            assert self.ev[name] == (cname, self.ctr[cname]), \
                f"pass mismatch at {name}: dry {self.ev[name]} real {(cname, self.ctr[cname])}"

    def wait(self, eng, name):
        if self.dry:
            return
        cname, v = self.ev[name]
        if v > 0:
            eng.wait_ge(self.sem[cname], v)

    def chain(self, eng, cname, inst):
        n = self.bump(cname, inst)
        if not self.dry:
            eng.wait_ge(self.sem[cname], n)
        return n


def build_program(debug=()):
    nc = bass.Bass("TRN2", target_bir_lowering=False)

    def dk(name):
        return "ExternalOutput" if name in debug else "Internal"

    xg = nc.dram_tensor("xg", [T, D], F32, kind="ExternalInput").ap()
    masks = nc.dram_tensor("masks", [128, 4096], BF16, kind="ExternalInput").ap()
    wqkv = nc.dram_tensor("wqkv", [D, 3 * D], F32, kind="ExternalInput").ap()
    bqkv = nc.dram_tensor("bqkv", [3 * D], F32, kind="ExternalInput").ap()
    wout = nc.dram_tensor("wout", [D, D], F32, kind="ExternalInput").ap()
    bout = nc.dram_tensor("bout", [D], F32, kind="ExternalInput").ap()
    w1 = nc.dram_tensor("w1", [8192, D], F32, kind="ExternalInput").ap()
    b1 = nc.dram_tensor("b1", [8192], F32, kind="ExternalInput").ap()
    w2 = nc.dram_tensor("w2", [D, 8192], F32, kind="ExternalInput").ap()
    b2 = nc.dram_tensor("b2", [D], F32, kind="ExternalInput").ap()
    y = nc.dram_tensor("y", [TO, D], F32, kind="ExternalOutput").ap()

    xln_hbm = nc.dram_tensor("xln_hbm", [T, D], BF16, kind=dk("xln_hbm")).ap()
    q_hbm = nc.dram_tensor("q_hbm", [D, TO], BF16, kind=dk("q_hbm")).ap()
    k_hbm = nc.dram_tensor("k_hbm", [D, T], BF16, kind=dk("k_hbm")).ap()
    ao_hbm = nc.dram_tensor("ao_hbm", [TO, D], BF16, kind=dk("ao_hbm")).ap()
    x2ln_hbm = nc.dram_tensor("x2ln_hbm", [TO, D], BF16, kind=dk("x2ln_hbm")).ap()
    w1q_hbm = nc.dram_tensor("w1q_hbm", [8192, D], BF16, kind=dk("w1q_hbm")).ap()
    w2q_hbm = nc.dram_tensor("w2q_hbm", [D, 8192], BF16, kind=dk("w2q_hbm")).ap()
    dbg_pt = None
    if "dbg_pt" in debug:
        dbg_pt = nc.dram_tensor("dbg_pt", [128, 4608], BF16, kind="ExternalOutput").ap()

    wqkv_r = wqkv.rearrange("(dc p) n -> p dc n", p=128)
    wout_r = wout.rearrange("(cc p) n -> p cc n", p=128)

    with contextlib.ExitStack() as stk:
        def sbt(name, shape, dt):
            return stk.enter_context(nc.sbuf_tensor(name, shape, dt))

        A32t = sbt("A32", [128, 16384], F32)
        A32 = A32t.ap()
        A16 = A32.bitcast(BF16)
        B16 = sbt("B16", [128, 33024], BF16).ap()
        C32 = sbt("C32", [128, 12288], F32).ap()
        C16 = C32.bitcast(BF16)
        p1buf = sbt("p1buf", [128, 2048], F32).ap()
        p1buf2 = sbt("p1buf2", [128, 1024], F32).ap()
        evq = sbt("evq", [128, 2048], BF16).ap()
        cst = sbt("cst", [128, 8], F32).ap()
        onesb = sbt("onesb", [128, 128], BF16).ap()
        ones32r = sbt("ones32r", [1, 128], F32).ap()
        bqcols = sbt("bqcols", [128, 32], F32).ap()
        qbias = sbt("qbias", [128, 16], F32).ap()
        b1cols = sbt("b1cols", [128, 64], F32).ap()
        brows = sbt("brows", [1, 6144], BF16).ap()
        statc = sbt("statc", [128, 64], F32).ap()
        srow = sbt("srow", [1, 64], F32).ap()
        pacc = sbt("pacc", [128, 256], F32).ap()

        psA = stk.enter_context(nc.psum_tensor("psA", [128, 2048], F32)).ap()
        psB = stk.enter_context(nc.psum_tensor("psB", [128, 2048], F32)).ap()

        X = Ctx()
        for cn in SEMS:
            X.sem[cn] = stk.enter_context(nc.semaphore(cn))

        xlnv = lambda dc: A16[:, dc * 2048:(dc + 1) * 2048]
        ao_tm = lambda tt: A16[:, tt * 2048:(tt + 1) * 2048]
        x2v = lambda tt: A32[:, tt * 2048:(tt + 1) * 2048]
        vslot = lambda h, s: B16[:, (h * NT + s) * 129:(h * NT + s) * 129 + 129]
        ao_fm = lambda cc: B16[:, cc * 1024:(cc + 1) * 1024]
        w1qTv = lambda dc: B16[:, 16384 + dc * 1024:16384 + (dc + 1) * 1024]
        w2qTv = lambda fc: B16[:, fc * 2048:(fc + 1) * 2048]
        xt = [C32[:, 0:2048], C32[:, 2048:4096]]
        sqf = [C32[:, 4096:6144], C32[:, 6144:8192]]
        lntm = [C16[:, 16384:18432], C16[:, 18432:20480]]
        qk32 = [C32[:, 0:4096], C32[:, 4096:8192]]
        qkbf = [C16[:, 16384:20480], C16[:, 20480:24576]]
        v32 = C32[:, 0:4096]
        vbf = [C16[:, 16384:20480], C16[:, 20480:24576]]
        kTb = [C16[:, 0:2048], C16[:, 2048:4096]]
        qTb = [C16[:, 4096:5120], C16[:, 5120:6144]]
        woutst = [C16[:, 0:8192], C16[:, 8192:16384]]
        x2lnfm = lambda dc: C16[:, dc * 1024:(dc + 1) * 1024]
        hgv = lambda fc: C16[:, 16384 + fc * 1024:16384 + (fc + 1) * 1024]
        ln2sq = C32[:, 4096:6144]
        p1s = [p1buf[:, 0:1024], p1buf[:, 1024:2048], p1buf2[:, 0:1024]]
        p1tv4 = [evq[:, 0:1024], evq[:, 1024:2048],
                 C16[:, 9216:10240], C16[:, 10240:11264]]
        NQIL = 192

        def ptb(u):
            return u % 4 if u < NQIL else u % 2

        def ptprev(u):
            return u - 4 if u < NQIL + 2 else u - 2
        PTv = [C16[:, 6144:7680], C16[:, 7680:9216]]
        msk = C16[:, 11264:15360]
        x2st = C16[:, 14336:16384]

        LN_S0, LN_S2, LN_MEAN, LN_E2, LN_VAR, LN_RSTD, LN_NMR = 0, 1, 2, 3, 4, 5, 6
        NEGC = 8
        T1C, NT1C, T2C, NT2C = 26, 27, 28, 29
        RECC = 30

        sched = [_chunks(KJ[j]) for j in range(NPAIR)]
        nchunk_pair = [len(s) for s in sched]
        gchunk_of = {}
        g = 0
        for h in range(H):
            for j in range(NPAIR):
                for ci in range(nchunk_pair[j]):
                    gchunk_of[(h, j, ci)] = g
                    g += 1
        NGCH = g

        def p0_sync(q):
            pass

        def p0_pool(gp):
            X.dma("iniP", gp.dma_start(out=bqcols, in_=bqkv[0:4096].rearrange("(m p) -> p m", p=128),
                                     allow_slow_non_contiguous=True))
            X.dma("iniP", gp.dma_start(out=b1cols, in_=b1.rearrange("(m p) -> p m", p=128),
                                     allow_slow_non_contiguous=True))
            X.dma("iniP", gp.dma_start(out=brows[0:1, 0:2048],
                                     in_=bqkv.rearrange("(a n) -> a n", a=3)[2:3, :]))
            X.dma("iniP", gp.dma_start(out=brows[0:1, 2048:4096],
                                     in_=bout.rearrange("(a n) -> a n", a=1)))
            X.dma("iniP", gp.dma_start(out=brows[0:1, 4096:6144],
                                     in_=b2.rearrange("(a n) -> a n", a=1)))
            X.mark("init_pool", "iniP")

        def p0_vec(v):
            v.memset(cst[:, 0:1], 1.0)
            v.memset(cst[:, 1:2], EPS)
            v.memset(cst[:, 2:3], QSCALE)
            v.memset(onesb, 1.0)
            v.memset(ones32r, 1.0)
            X.bump("cV", v.memset(B16[:, 128:H * NT * 129:129], 1.0))
            X.wait(v, "init_pool")
            X.bump("cV", v.tensor_scalar(out=qbias, in0=bqcols[:, 0:16], scalar1=QSCALE,
                                         scalar2=None, op0=ALU.mult))
            X.mark("init_vec", "cV")

        def p1_sync(q):
            for i in range(NT):
                if i >= 2:
                    X.wait(q, f"ln1_norm_{i-2}")
                X.dma(f"lx{i % 2}", q.dma_start(out=xt[i % 2], in_=xg[i * 128:(i + 1) * 128, :]))
                X.mark(f"ln1_ld_{i}", f"lx{i % 2}")

        def p1_scalar(s):
            for i in range(NT):
                X.wait(s, f"ln1_ld_{i}")
                if i >= 1:
                    X.wait(s, f"ln1_var_{i-1}")
                X.chain(s, "cS", s.activation(out=sqf[i % 2], in_=xt[i % 2], func=AF.Identity,
                                              accum_out=statc[:, LN_S0:LN_S0 + 1]))
                X.bump("cS", s.activation(out=sqf[i % 2], in_=xt[i % 2], func=AF.Square,
                                          accum_out=statc[:, LN_S2:LN_S2 + 1]))
                X.mark(f"ln1_sums_{i}", "cS")
                X.wait(s, f"ln1_var_{i}")
                X.chain(s, "cS", s.activation(out=statc[:, LN_RSTD:LN_RSTD + 1],
                                              in_=statc[:, LN_VAR:LN_VAR + 1], func=AF.Ln,
                                              bias=cst[:, 1:2]))
                X.chain(s, "cS", s.activation(out=statc[:, LN_RSTD:LN_RSTD + 1],
                                              in_=statc[:, LN_RSTD:LN_RSTD + 1], func=AF.Exp,
                                              scale=-0.5))
                X.mark(f"ln1_rstd_{i}", "cS")
                X.wait(s, f"ln1_nmr_{i}")
                if i >= 2:
                    X.wait(s, f"ln1_sp_{i-2}")
                X.chain(s, "cS", s.activation(out=lntm[i % 2], in_=xt[i % 2], func=AF.Identity,
                                              scale=statc[:, LN_RSTD:LN_RSTD + 1],
                                              bias=statc[:, LN_NMR:LN_NMR + 1]))
                X.mark(f"ln1_norm_{i}", "cS")
                X.dma(f"xs{i % 2}", s.dma_start(out=xln_hbm[i * 128:(i + 1) * 128, :], in_=lntm[i % 2]))
                X.mark(f"ln1_sp_{i}", f"xs{i % 2}")
            X.wait(s, "ln1_sp_14")
            X.wait(s, "ln1_sp_15")
            for dc in range(NT):
                X.dma("xlx", s.dma_start(out=xlnv(dc), in_=xln_hbm[:, dc * 128:(dc + 1) * 128],
                                        transpose=True))
            X.mark("xln_fm", "xlx")

        def p1_vec(v):
            for i in range(NT):
                X.wait(v, f"ln1_sums_{i}")
                X.chain(v, "cV", v.tensor_scalar(out=statc[:, LN_MEAN:LN_MEAN + 1],
                                                 in0=statc[:, LN_S0:LN_S0 + 1], scalar1=INV_D,
                                                 scalar2=None, op0=ALU.mult))
                X.chain(v, "cV", v.tensor_scalar(out=statc[:, LN_E2:LN_E2 + 1],
                                                 in0=statc[:, LN_S2:LN_S2 + 1], scalar1=INV_D,
                                                 scalar2=None, op0=ALU.mult))
                X.chain(v, "cV", v.tensor_mul(out=statc[:, LN_VAR:LN_VAR + 1],
                                              in0=statc[:, LN_MEAN:LN_MEAN + 1],
                                              in1=statc[:, LN_MEAN:LN_MEAN + 1]))
                X.chain(v, "cV", v.tensor_sub(out=statc[:, LN_VAR:LN_VAR + 1],
                                              in0=statc[:, LN_E2:LN_E2 + 1],
                                              in1=statc[:, LN_VAR:LN_VAR + 1]))
                X.mark(f"ln1_var_{i}", "cV")
                X.wait(v, f"ln1_rstd_{i}")
                X.chain(v, "cV", v.tensor_scalar(out=statc[:, LN_NMR:LN_NMR + 1],
                                                 in0=statc[:, LN_MEAN:LN_MEAN + 1],
                                                 scalar1=statc[:, LN_RSTD:LN_RSTD + 1],
                                                 scalar2=-1.0, op0=ALU.mult, op1=ALU.mult))
                X.mark(f"ln1_nmr_{i}", "cV")
            X.mark("ln1_vdone", "cV")

        def p1src(u):
            if u < NQU:
                r, cp = u // 2, u % 2
                return w1[r * 128:(r + 1) * 128, cp * 1024:(cp + 1) * 1024]
            r, cp = (u - NQU) // 8, (u - NQU) % 8
            return w2[r * 128:(r + 1) * 128, cp * 1024:(cp + 1) * 1024]

        def pqdst(u):
            if u < NQU:
                r, cp = u // 2, u % 2
                return w1q_hbm[r * 128:(r + 1) * 128, cp * 1024:(cp + 1) * 1024]
            r, cp = (u - NQU) // 8, (u - NQU) % 8
            return w2q_hbm[r * 128:(r + 1) * 128, cp * 1024:(cp + 1) * 1024]

        def p2_sync(q):
            X.wait(q, "ln1_norm_15")
            X.wait(q, "ln1_vdone")
            for r in range(16):
                if r >= 2:
                    X.wait(q, f"qk_conv_{r-2}")
                c0 = r * 256 if r < 8 else 2048 + (r - 8) * 256
                for d4 in range(4):
                    X.dma(f"qw{r % 2}", q.dma_start(
                        out=qk32[r % 2][:, d4 * 1024:(d4 + 1) * 1024]
                            .rearrange("p (dc n) -> p dc n", n=256),
                        in_=wqkv_r[:, d4 * 4:(d4 + 1) * 4, c0:c0 + 256]))
                X.mark(f"qk_ld_{r}", f"qw{r % 2}")
                for k in range(16):
                    u = r * 16 + k
                    if u >= 3:
                        X.wait(q, f"p1red_{u-3}")
                    X.dma(f"lq{u % 3}", q.dma_start(out=p1s[u % 3], in_=p1src(ORD[u])))
                    X.mark(f"p1ld_{u}", f"lq{u % 3}")
            for vb in range(8):
                if vb == 0:
                    X.wait(q, "qk_conv_14")
                if vb >= 1:
                    X.wait(q, f"v_conv_{vb-1}")
                c0 = 4096 + vb * 256
                for d4 in range(4):
                    X.dma("vw", q.dma_start(
                        out=v32[:, d4 * 1024:(d4 + 1) * 1024]
                            .rearrange("p (dc n) -> p dc n", n=256),
                        in_=wqkv_r[:, d4 * 4:(d4 + 1) * 4, c0:c0 + 256]))
                X.mark(f"v_ld_{vb}", "vw")

        def p2_tensor(t):
            X.wait(t, "xln_fm")
            if True:
                for m in range(32):
                    X.wait(t, f"qk_conv_{m // 2}")
                    if m >= 2:
                        X.wait(t, f"qk_evd_{m-2}")
                    ps = psA if m % 2 == 0 else psB
                    inst = None
                    ntt = 2 if m < 16 else 4
                    for dc in range(16):
                        lhs = qkbf[(m // 2) % 2][:, dc * 256 + (m % 2) * 128:
                                                 dc * 256 + (m % 2) * 128 + 128]
                        for t4 in range(ntt):
                            inst = t.matmul(ps[:, t4 * 512:(t4 + 1) * 512], lhs,
                                            xlnv(dc)[:, t4 * 512:(t4 + 1) * 512],
                                            start=(dc == 0), stop=(dc == 15))
                    X.bump("cT", inst)
                    X.mark(f"qk_mm_{m}", "cT")
            X.wait(t, "init_pool")
            for vb in range(8):
                X.wait(t, f"v_conv_{vb}")
                for tt in range(16):
                    u = vb * 16 + tt
                    bank = tt % 8
                    ps = psA if bank < 4 else psB
                    po = ps[:, (bank % 4) * 512:(bank % 4) * 512 + 256]
                    if u == 0:
                        X.wait(t, "qk_evd_30")
                    if u == 4:
                        X.wait(t, "qk_evd_31")
                    if u >= 8:
                        X.wait(t, f"v_ev_{u-8}")
                    for dc in range(16):
                        t.matmul(po, xlnv(dc)[:, tt * 128:(tt + 1) * 128],
                                 vbf[vb % 2][:, dc * 256:(dc + 1) * 256],
                                 start=(dc == 0), stop=False)
                    inst = t.matmul(po, onesb[0:1, :], brows[0:1, vb * 256:vb * 256 + 256],
                                    start=False, stop=True)
                    X.bump("cT", inst)
                    X.mark(f"v_mm_{u}", "cT")
            X.mark("qkv_done", "cT")

        def p2_vec(v):
            for r in range(16):
                X.wait(v, f"qk_ld_{r}")
                if r == 0:
                    X.wait(v, "ln1_sp_14")
                    X.wait(v, "ln1_sp_15")
                if r >= 2:
                    X.wait(v, f"qk_mm_{2 * r - 3}")
                X.bump("cV", v.tensor_copy(out=qkbf[r % 2], in_=qk32[r % 2]))
                X.mark(f"qk_conv_{r}", "cV")
        def conv_vb(v, vb):
            X.wait(v, f"v_ld_{vb}")
            if vb == 0:
                X.wait(v, "qk_mm_31")
            if vb >= 2:
                X.wait(v, f"v_mm_{(vb-2)*16+15}")
            X.bump("cV", v.tensor_copy(out=vbf[vb % 2], in_=v32))
            X.mark(f"v_conv_{vb}", "cV")

        def p2_vec_b(v):
            conv_vb(v, 0)
            for vb in range(8):
                if vb + 1 < 8:
                    conv_vb(v, vb + 1)
                for tt in range(16):
                    u = vb * 16 + tt
                    X.wait(v, f"v_mm_{u}")
                    bank = tt % 8
                    ps = psA if bank < 4 else psB
                    po = ps[:, (bank % 4) * 512:(bank % 4) * 512 + 256]
                    X.bump("cV", v.tensor_copy(out=vslot(2 * vb, tt)[:, 0:128], in_=po[:, 0:128]))
                    X.bump("cV", v.tensor_copy(out=vslot(2 * vb + 1, tt)[:, 0:128],
                                               in_=po[:, 128:256]))
                    X.mark(f"v_ev_{u}", "cV")
            X.mark("v_all", "cV")

        def p2_scalar(s):
            stp = {"u": 0}

            def p1red(k):
                for _ in range(k):
                    u = stp["u"]
                    if u >= 2 * NQU:
                        return
                    X.wait(s, f"p1ld_{u}")
                    lu = ORD[u]
                    X.bump("cS", s.activation(out=p1s[u % 3], in_=p1s[u % 3], func=AF.Abs,
                                              accum_out=pacc[:, lu:lu + 1]))
                    X.mark(f"p1red_{u}", "cS")
                    stp["u"] += 1
            for m in range(32):
                p1red(8)
                X.wait(s, f"qk_mm_{m}")
                if m >= 1:
                    X.wait(s, f"qk_sp_{m-1}")
                ps = psA if m % 2 == 0 else psB
                if m < 16:
                    for t4 in range(2):
                        X.bump("cS", s.activation(out=evq[:, t4 * 512:(t4 + 1) * 512],
                                                  in_=ps[:, t4 * 512:(t4 + 1) * 512],
                                                  func=AF.Identity, scale=cst[:, 2:3],
                                                  bias=qbias[:, m:m + 1]))
                    X.mark(f"qk_evd_{m}", "cS")
                    X.wait(s, f"qk_evd_{m}")
                    X.dma("qsp", s.dma_start(out=q_hbm[m * 128:(m + 1) * 128, :],
                                            in_=evq[:, 0:1024]))
                else:
                    for t4 in range(4):
                        X.bump("cS", s.activation(out=evq[:, t4 * 512:(t4 + 1) * 512],
                                                  in_=ps[:, t4 * 512:(t4 + 1) * 512],
                                                  func=AF.Identity, scale=cst[:, 0:1],
                                                  bias=bqcols[:, m:m + 1]))
                    X.mark(f"qk_evd_{m}", "cS")
                    X.wait(s, f"qk_evd_{m}")
                    X.dma("qsp", s.dma_start(out=k_hbm[(m - 16) * 128:(m - 15) * 128, :],
                                            in_=evq[:, 0:2048]))
                X.mark(f"qk_sp_{m}", "qsp")
            p1red(2 * NQU)
            X.mark("p1_all", "cS")
            X.mark("qkv_spills", "qsp")


        def p2c_sync(q):
            X.wait(q, "qkv_spills")
            X.wait(q, "v_conv_7")
            for h in range(H):
                if h >= 2:
                    X.wait(q, f"bnred_{h-2}")
                X.dma(f"kq{h % 2}", q.dma_start(out=kTb[h % 2], in_=k_hbm[h * 128:(h + 1) * 128, :]))
                X.dma(f"kq{h % 2}", q.dma_start(out=qTb[h % 2], in_=q_hbm[h * 128:(h + 1) * 128, :]))
                X.mark(f"bn_ld_{h}", f"kq{h % 2}")

        def p2c_scalar(s):
            X.wait(s, "qkv_spills")
            for h in range(H):
                X.wait(s, f"bn_ld_{h}")
                if h >= 1:
                    X.wait(s, f"bnq_{h-1}")
                X.bump("cS", s.square(out=evq[:, 0:2048], in_=kTb[h % 2]))
                X.mark(f"bnsqk_{h}", "cS")
                X.wait(s, f"bnk_{h}")
                X.bump("cS", s.square(out=evq[:, 0:1024], in_=qTb[h % 2]))
                X.mark(f"bnsqq_{h}", "cS")
            X.wait(s, "bnred_all")
            X.chain(s, "cS", s.activation(out=srow[0:1, 32:64], in_=srow[0:1, 32:64], func=AF.Ln))
            X.mark("bnln", "cS")
            X.wait(s, "bnadd")
            X.chain(s, "cS", s.activation(out=srow[0:1, 32:48], in_=srow[0:1, 32:48], func=AF.Exp,
                                          scale=0.5))
            X.mark("bnexp", "cS")

        def p2c_tensor(t):
            X.wait(t, "v_all")
            for h in range(H):
                X.wait(t, f"bnsqk_{h}")
                if h >= 1:
                    X.wait(t, f"bnred_{h-1}")
                inst = None
                for p_ in range(4):
                    inst = t.matmul(psA[0:1, p_ * 512:(p_ + 1) * 512], onesb[:, 0:1],
                                    evq[:, p_ * 512:(p_ + 1) * 512], start=True, stop=True)
                X.bump("cT", inst)
                X.mark(f"bnk_{h}", "cT")
                X.wait(t, f"bnsqq_{h}")
                inst = t.matmul(psB[0:1, 0:512], onesb[:, 0:1], evq[:, 0:512],
                                start=True, stop=True)
                inst = t.matmul(psB[0:1, 512:1024], onesb[:, 0:1], evq[:, 512:1024],
                                start=True, stop=True)
                X.bump("cT", inst)
                X.mark(f"bnq_{h}", "cT")
            X.wait(t, "bnneg")
            X.bump("cT", t.matmul(psB[:, 0:16], ones32r, srow[0:1, 32:48],
                                  start=True, stop=True))
            X.mark("bnt", "cT")

        def p2c_vec(v):
            for h in range(H):
                X.wait(v, f"bnk_{h}")
                X.chain(v, "cV", v.reduce_max(out=srow[0:1, 32 + h:33 + h],
                                              in_=psA[0:1, 0:2048], axis=AX.X))
                X.wait(v, f"bnq_{h}")
                X.chain(v, "cV", v.reduce_max(out=srow[0:1, 48 + h:49 + h],
                                              in_=psB[0:1, 0:1024], axis=AX.X))
                X.mark(f"bnred_{h}", "cV")
            X.mark("bnred_all", "cV")
            X.wait(v, "bnln")
            X.chain(v, "cV", v.tensor_add(out=srow[0:1, 32:48], in0=srow[0:1, 32:48],
                                          in1=srow[0:1, 48:64]))
            X.mark("bnadd", "cV")
            X.wait(v, "bnexp")
            X.chain(v, "cV", v.tensor_scalar(out=srow[0:1, 32:48], in0=srow[0:1, 32:48],
                                             scalar1=-1.0, scalar2=-1.0,
                                             op0=ALU.mult, op1=ALU.add))
            X.mark("bnneg", "cV")
            X.wait(v, "bnt")
            X.chain(v, "cV", v.tensor_copy(out=statc[:, 8:24], in_=psB[:, 0:16]))
            X.mark("negc_all", "cV")

        QS = {"ld": 0, "sp": 0, "q": 0, "g": 0}

        def qld_issue(q, k):
            for _ in range(k):
                u = QS["ld"]
                if u >= 2 * NQU:
                    return
                if u < 3:
                    X.wait(q, "p1_all")
                else:
                    X.wait(q, f"wqf_{u-3}")
                X.dma(f"lq{u % 3}", q.dma_start(out=p1s[u % 3], in_=p1src(ORD[u])))
                X.mark(f"wqld_{u}", f"lq{u % 3}")
                QS["ld"] += 1

        def qsp_issue(q, k):
            for _ in range(k):
                u = QS["sp"]
                if u >= 2 * NQU:
                    return
                X.wait(q, f"wqf_{u}")
                X.dma(f"ws{ptb(u)}", q.dma_start(out=pqdst(ORD[u]), in_=p1tv4[ptb(u)]))
                X.mark(f"wqsp_{u}", f"ws{ptb(u)}")
                if (u + 1) % 32 == 0:
                    g_ = (u + 1) // 32 - 1
                    for sw in range(4):
                        X.mark(f"qgrp_{g_}_{sw}", f"ws{sw}")
                QS["sp"] += 1

        def quant(v, k):
            for _ in range(k):
                u = QS["q"]
                if u >= 2 * NQU:
                    return
                if u == 0:
                    X.wait(v, "qkv_spills")
                    X.wait(v, "bnq_15")
                X.wait(v, f"wqld_{u}")
                pu = ptprev(u)
                if pu >= 0:
                    X.wait(v, f"wqsp_{pu}")
                lu = ORD[u]
                tc, ntc = (T1C, NT1C) if lu < NQU else (T2C, NT2C)
                X.chain(v, "cV", v.tensor_scalar(out=p1tv4[ptb(u)], in0=p1s[u % 3],
                                                 scalar1=statc[:, ntc:ntc + 1],
                                                 scalar2=None, op0=ALU.is_le))
                X.bump("cV", v.scalar_tensor_tensor(out=p1tv4[ptb(u)], in0=p1s[u % 3],
                                                    scalar=statc[:, tc:tc + 1],
                                                    in1=p1tv4[ptb(u)],
                                                    op0=ALU.is_ge, op1=ALU.subtract))
                X.mark(f"wqf_{u}", "cV")
                QS["q"] += 1

        def p3_sync(q):
            X.wait(q, "qkv_spills")
            X.wait(q, "v_all")
            X.wait(q, "bnexp")
            X.dma("mskq", q.dma_start(out=msk, in_=masks))
            X.mark("msk_ld", "mskq")
            def kqld(h):
                if h >= 2:
                    X.wait(q, f"head_done_{h-2}")
                X.dma(f"kq{h % 2}", q.dma_start(out=kTb[h % 2][0:64, :],
                                                in_=k_hbm[h * 128:h * 128 + 64, :]))
                X.dma(f"kq{h % 2}", q.dma_start(out=kTb[h % 2][64:128, :],
                                                in_=k_hbm[h * 128 + 64:(h + 1) * 128, :]))
                X.dma(f"kq{h % 2}", q.dma_start(out=qTb[h % 2], in_=q_hbm[h * 128:(h + 1) * 128, :]))
                X.mark(f"kq_ld_{h}", f"kq{h % 2}")
            kqld(0)
            kqld(1)
            for h in range(H):
                if h + 2 < H:
                    kqld(h + 2)
                qld_issue(q, 12)

        def p3_pool(gp):
            qsp_issue(gp, NQIL)

        def p3_tensor(t):
            def thr_w1(t):
                X.wait(t, "thrv_w1_a")
                inst = t.matmul(psA[0:1, 0:1], cst[:, 0:1], statc[:, 32:33],
                                start=True, stop=True)
                inst = t.matmul(psA[0:1, 1:2], cst[:, 0:1], statc[:, 33:34],
                                start=True, stop=True)
                X.bump("cT", inst)
                X.mark("thrt_w1_a", "cT")
                X.wait(t, "thrv_w1_b")
                inst = t.matmul(psA[:, 0:1], ones32r, srow[0:1, 18:19], start=True, stop=True)
                inst = t.matmul(psA[:, 1:2], ones32r, srow[0:1, 19:20], start=True, stop=True)
                X.bump("cT", inst)
                X.mark("thrt_w1_b", "cT")
                X.wait(t, "thrcp_w1")

            def thr_w2(t):
                X.wait(t, "thrv_w2_a")
                inst = t.matmul(psA[0:1, 0:1], cst[:, 0:1], statc[:, 34:35],
                                start=True, stop=True)
                inst = t.matmul(psA[0:1, 1:2], cst[:, 0:1], statc[:, 35:36],
                                start=True, stop=True)
                X.bump("cT", inst)
                X.mark("thrt_w2_a", "cT")
                X.wait(t, "thrv_w2_b")
                inst = t.matmul(psA[:, 0:1], ones32r, srow[0:1, 18:19], start=True, stop=True)
                inst = t.matmul(psA[:, 1:2], ones32r, srow[0:1, 19:20], start=True, stop=True)
                X.bump("cT", inst)
                X.mark("thrt_w2_b", "cT")
                X.wait(t, "thrcp_w2")

            def emit_sc(t, it):
                h, j, ci, c0, n, K, kl = it
                gch = gchunk_of[(h, j, ci)]
                par = gch % 2
                psS = psA if par == 0 else psB
                if j == 0 and ci == 0:
                    X.wait(t, f"kq_ld_{h}")
                if gch >= 2:
                    X.wait(t, f"exp_{gch-2}")
                inst = None
                for u in range(n):
                    s_ = kl[c0 + u]
                    inst = t.matmul(psS[:, u * 256:(u + 1) * 256],
                                    kTb[h % 2][:, s_ * 128:(s_ + 1) * 128],
                                    qTb[h % 2][:, j * 256:(j + 1) * 256],
                                    start=True, stop=True)
                X.bump("cT", inst)
                X.mark(f"sc_{gch}", "cT")

            def emit_av(t, it):
                h, j, ci, c0, n, K, kl = it
                gch = gchunk_of[(h, j, ci)]
                par = gch % 2
                X.wait(t, f"exp_{gch}")
                if (h, j, ci) in mask_ev:
                    X.wait(t, mask_ev[(h, j, ci)])
                if ci == 0:
                    X.wait(t, f"aoev_{h}_{j-1}" if j > 0 else
                           (f"aoev_{h-1}_{NPAIR-1}" if h > 0 else "negc_all"))
                inst = None
                for u in range(n):
                    pos = c0 + u
                    s_ = kl[pos]
                    for qt in range(2):
                        pav = psA if qt == 0 else psB
                        inst = t.matmul(pav[:, 1536:1665],
                                        PTv[par][:, u * 256 + qt * 128:u * 256 + qt * 128 + 128],
                                        vslot(h, s_),
                                        start=(pos == 0),
                                        stop=(pos == K - 1))
                X.bump("cT", inst)
                X.mark(f"av_{gch}", "cT")
                if j == NPAIR - 1 and ci == nchunk_pair[j] - 1:
                    X.mark(f"head_done_{h}", "cT")

            X.wait(t, "negc_all")
            thr_w1(t)
            thr_w2(t)
            SKEW = 2
            seq = []
            for h in range(H):
                for j in range(NPAIR):
                    K = KJ[j]
                    kl = KLIST[K]
                    for ci, (c0, n) in enumerate(sched[j]):
                        seq.append((h, j, ci, c0, n, K, kl))
            for idx, it in enumerate(seq):
                emit_sc(t, it)
                if idx >= SKEW:
                    emit_av(t, seq[idx - SKEW])
            for it in seq[-SKEW:]:
                emit_av(t, it)
            X.mark("attn_done", "cT")

        def p3_scalar(s):
            for h in range(H):
                for j in range(NPAIR):
                    for ci, (c0, n) in enumerate(sched[j]):
                        gch = gchunk_of[(h, j, ci)]
                        par = gch % 2
                        psS = psA if par == 0 else psB
                        X.wait(s, f"sc_{gch}")
                        if gch >= 2:
                            X.wait(s, f"av_{gch-2}")
                        X.bump("cS", s.activation(out=PTv[par][:, 0:n * 256],
                                                  in_=psS[:, 0:n * 256], func=AF.Exp,
                                                  bias=statc[:, 8 + h:9 + h]))
                        X.mark(f"exp_{gch}", "cS")
            X.mark("attn_s_done", "cS")

        mask_ev = {}

        srowI = srow.bitcast(I32)

        def emit_thr_dekker(v):
            sh = psA[0:1, 0:1]
            sl = psA[0:1, 1:2]
            c = lambda i: srow[0:1, i:i + 1]
            ci = lambda i: srowI[0:1, i:i + 1]
            ch = X.chain
            ch(v, "cV", v.tensor_scalar(out=c(8), in0=sh, scalar1=2.0 ** -12, scalar2=SPL_C,
                                        op0=ALU.mult, op1=ALU.add))
            ch(v, "cV", v.tensor_scalar(out=c(8), in0=c(8), scalar1=-SPL_C, scalar2=4096.0,
                                        op0=ALU.add, op1=ALU.mult))
            ch(v, "cV", v.scalar_tensor_tensor(out=c(10), in0=c(8), scalar=-1.0, in1=sh,
                                               op0=ALU.mult, op1=ALU.add))
            ch(v, "cV", v.tensor_scalar(out=c(11), in0=sh, scalar1=CH, scalar2=None,
                                        op0=ALU.mult))
            ch(v, "cV", v.tensor_scalar(out=c(15), in0=c(8), scalar1=CH_A, scalar2=None,
                                        op0=ALU.mult))
            ch(v, "cV", v.tensor_sub(out=c(12), in0=c(15), in1=c(11)))
            for src, k in ((8, CH_B), (10, CH_A), (10, CH_B)):
                ch(v, "cV", v.tensor_scalar(out=c(15), in0=c(src), scalar1=k, scalar2=None,
                                            op0=ALU.mult))
                ch(v, "cV", v.tensor_add(out=c(12), in0=c(12), in1=c(15)))
            ch(v, "cV", v.tensor_scalar(out=c(15), in0=sh, scalar1=CL, scalar2=None,
                                        op0=ALU.mult))
            ch(v, "cV", v.tensor_add(out=c(12), in0=c(12), in1=c(15)))
            ch(v, "cV", v.tensor_scalar(out=c(15), in0=sl, scalar1=CH, scalar2=None,
                                        op0=ALU.mult))
            ch(v, "cV", v.tensor_add(out=c(12), in0=c(12), in1=c(15)))
            ch(v, "cV", v.tensor_add(out=c(13), in0=c(11), in1=c(12)))
            ch(v, "cV", v.tensor_sub(out=c(15), in0=c(13), in1=c(11)))
            ch(v, "cV", v.tensor_sub(out=c(14), in0=c(12), in1=c(15)))
            ch(v, "cV", v.tensor_scalar(out=c(15), in0=c(14), scalar1=0.0, scalar2=None,
                                        op0=ALU.is_gt))
            ch(v, "cV", v.tensor_scalar(out=c(16), in0=c(13), scalar1=ULP_PUSH, scalar2=None,
                                        op0=ALU.mult))
            ch(v, "cV", v.tensor_mul(out=c(16), in0=c(16), in1=c(15)))
            ch(v, "cV", v.tensor_add(out=c(18), in0=c(13), in1=c(16)))
            ch(v, "cV", v.tensor_scalar(out=c(19), in0=c(18), scalar1=-1.0, scalar2=None,
                                        op0=ALU.mult))

        def p3_vec(v):
            X.wait(v, "p1_all")
            for lv in [64, 32, 16, 8, 4, 2, 1]:
                X.chain(v, "cV", v.tensor_add(out=pacc[:, 0:lv], in0=pacc[:, 0:lv],
                                              in1=pacc[:, lv:2 * lv]))
            X.chain(v, "cV", v.tensor_scalar(out=statc[:, 36:37], in0=pacc[:, 0:1],
                                             scalar1=RND_C, scalar2=None, op0=ALU.add))
            X.chain(v, "cV", v.tensor_scalar(out=statc[:, 32:33], in0=statc[:, 36:37],
                                             scalar1=-RND_C, scalar2=None, op0=ALU.add))
            X.chain(v, "cV", v.tensor_sub(out=statc[:, 33:34], in0=pacc[:, 0:1],
                                          in1=statc[:, 32:33]))
            X.mark("thrv_w1_a", "cV")
            X.wait(v, "thrt_w1_a")
            emit_thr_dekker(v)
            X.mark("thrv_w1_b", "cV")
            X.wait(v, "thrt_w1_b")
            X.chain(v, "cV", v.tensor_copy(out=statc[:, T1C:T1C + 1], in_=psA[:, 0:1]))
            X.chain(v, "cV", v.tensor_copy(out=statc[:, NT1C:NT1C + 1], in_=psA[:, 1:2]))
            X.mark("thrcp_w1", "cV")
            for lv in [64, 32, 16, 8, 4, 2, 1]:
                X.chain(v, "cV", v.tensor_add(out=pacc[:, 128:128 + lv],
                                              in0=pacc[:, 128:128 + lv],
                                              in1=pacc[:, 128 + lv:128 + 2 * lv]))
            X.chain(v, "cV", v.tensor_scalar(out=statc[:, 36:37], in0=pacc[:, 128:129],
                                             scalar1=RND_C, scalar2=None, op0=ALU.add))
            X.chain(v, "cV", v.tensor_scalar(out=statc[:, 34:35], in0=statc[:, 36:37],
                                             scalar1=-RND_C, scalar2=None, op0=ALU.add))
            X.chain(v, "cV", v.tensor_sub(out=statc[:, 35:36], in0=pacc[:, 128:129],
                                          in1=statc[:, 34:35]))
            X.mark("thrv_w2_a", "cV")
            X.wait(v, "thrt_w2_a")
            emit_thr_dekker(v)
            X.mark("thrv_w2_b", "cV")
            X.wait(v, "thrt_w2_b")
            X.chain(v, "cV", v.tensor_copy(out=statc[:, T2C:T2C + 1], in_=psA[:, 0:1]))
            X.chain(v, "cV", v.tensor_copy(out=statc[:, NT2C:NT2C + 1], in_=psA[:, 1:2]))
            X.mark("thrcp_w2", "cV")
            for h in range(H):
                for j in range(NPAIR):
                    K = KJ[j]
                    kl_j = KLIST[K]
                    mpos = [kl_j.index(L) for L in MLOC[j]]
                    for ci, (c0, n) in enumerate(sched[j]):
                        runs = []
                        for si, p_ in enumerate(mpos):
                            if c0 <= p_ < c0 + n:
                                if runs and runs[-1][1] == p_ and runs[-1][3] == si:
                                    runs[-1] = (runs[-1][0], p_ + 1, runs[-1][2], si + 1)
                                else:
                                    runs.append((p_, p_ + 1, si, si + 1))
                        if not runs:
                            continue
                        gch = gchunk_of[(h, j, ci)]
                        par = gch % 2
                        if h == 0:
                            X.wait(v, "msk_ld")
                        X.wait(v, f"exp_{gch}")
                        for (p0, p1, s0, s1) in runs:
                            X.bump("cV", v.tensor_mul(
                                out=PTv[par][:, (p0 - c0) * 256:(p1 - c0) * 256],
                                in0=PTv[par][:, (p0 - c0) * 256:(p1 - c0) * 256],
                                in1=msk[:, (j * 4 + s0) * 256:(j * 4 + s1) * 256]))
                        ev = f"mskd_{h}_{j}_{ci}"
                        X.mark(ev, "cV")
                        if X.dry:
                            mask_ev[(h, j, ci)] = ev
                    lastg = gchunk_of[(h, j, nchunk_pair[j] - 1)]
                    X.wait(v, f"av_{lastg}")
                    X.chain(v, "cV", v.reciprocal(out=statc[:, RECC:RECC + 1], in_=psA[:, 1664:1665]))
                    X.bump("cV", v.tensor_scalar(out=ao_tm(2 * j)[:, h * 128:(h + 1) * 128],
                                                 in0=psA[:, 1536:1664],
                                                 scalar1=statc[:, RECC:RECC + 1], scalar2=None,
                                                 op0=ALU.mult))
                    X.chain(v, "cV", v.reciprocal(out=statc[:, RECC + 1:RECC + 2], in_=psB[:, 1664:1665]))
                    X.bump("cV", v.tensor_scalar(out=ao_tm(2 * j + 1)[:, h * 128:(h + 1) * 128],
                                                 in0=psB[:, 1536:1664],
                                                 scalar1=statc[:, RECC + 1:RECC + 2], scalar2=None,
                                                 op0=ALU.mult))
                    X.mark(f"aoev_{h}_{j}", "cV")
                quant(v, 12)
            X.mark("attn_v_done", "cV")

        def p4_scalar(s):
            X.wait(s, "attn_v_done")
            for t8 in range(8):
                X.dma("aoq", s.dma_start(out=ao_hbm[t8 * 128:(t8 + 1) * 128, :],
                                        in_=A16[:, t8 * 2048:(t8 + 1) * 2048]))
            X.mark("ao_spill", "aoq")
            X.wait(s, "ao_spill")
            X.wait(s, "attn_done")
            for cc in range(16):
                X.dma("aoq", s.dma_start(out=ao_fm(cc), in_=ao_hbm[:, cc * 128:(cc + 1) * 128],
                                        transpose=True))
            X.mark("ao_fm", "aoq")
            for sw in range(4):
                X.wait(s, f"qgrp_0_{sw}")
            for dc in range(16):
                X.dma("w1x", s.dma_start(
                    out=w1qTv(dc),
                    in_=w1q_hbm[0:GF, dc * 128:(dc + 1) * 128],
                    transpose=True))
            X.mark("w1x_0", "w1x")

        def p4_sync(q):
            X.wait(q, "ao_spill")
            for k4 in range(4):
                X.dma("x2q", q.dma_start(
                    out=A32[:, k4 * 4096:(k4 + 1) * 4096].rearrange("p (tt d) -> p tt d", d=2048),
                    in_=xg[k4 * 256:(k4 + 1) * 256, :].rearrange("(tt p) d -> p tt d", p=128)))
            X.mark("x2_init", "x2q")
            while QS["ld"] < 2 * NQU:
                qld_issue(q, 4)
            for b_ in range(3):
                X.mark(f"lqfin_{b_}", f"lq{b_}")

        def p4_pool(gp):
            X.wait(gp, "attn_done")
            X.wait(gp, f"wqsp_{NQIL - 2}")
            X.wait(gp, f"wqsp_{NQIL - 1}")
            for db in range(4):
                if db >= 2:
                    X.wait(gp, f"op_mm_{db-2}_7")
                for cc in range(16):
                    X.dma(f"wo{db % 2}", gp.dma_start(out=woutst[db % 2][:, cc * 512:(cc + 1) * 512],
                                             in_=wout_r[:, cc, db * 512:(db + 1) * 512]))
                X.mark(f"wout_{db}", f"wo{db % 2}")
                if db >= 1:
                    qsp_issue(gp, 8)
            qsp_issue(gp, 2 * NQU)
            for sw in range(4):
                X.mark(f"wsfin_{sw}", f"ws{sw}")

        def p4_tensor(t):
            def pov(tt):
                ps = psA if tt % 8 < 4 else psB
                return ps[:, (tt % 4) * 512:(tt % 4) * 512 + 512]
            X.wait(t, "attn_s_done")
            X.wait(t, "attn_v_done")
            X.wait(t, "ao_fm")
            for db in range(4):
                X.wait(t, f"wout_{db}")
                if db >= 1:
                    for tt in range(8):
                        X.wait(t, f"op_ev_{(db - 1) * 8 + tt}")
                for cc in range(16):
                    for tt in range(8):
                        t.matmul(pov(tt), ao_fm(cc)[:, tt * 128:(tt + 1) * 128],
                                 woutst[db % 2][:, cc * 512:(cc + 1) * 512],
                                 start=(cc == 0), stop=False)
                X.wait(t, "init_pool")
                for tt in range(8):
                    inst = t.matmul(pov(tt), onesb[0:1, :],
                                    brows[0:1, 2048 + db * 512:2560 + db * 512],
                                    start=False, stop=True)
                    X.bump("cT", inst)
                    X.mark(f"op_mm_{db}_{tt}", "cT")
            X.mark("op_done", "cT")

        def p4_vec(v):
            X.wait(v, "x2_init")
            for db in range(4):
                for tt in range(8):
                    u = db * 8 + tt
                    bank = u % 8
                    ps = psA if bank < 4 else psB
                    po = ps[:, (bank % 4) * 512:(bank % 4) * 512 + 512]
                    X.wait(v, f"op_mm_{db}_{tt}")
                    X.bump("cV", v.tensor_add(out=x2v(tt)[:, db * 512:(db + 1) * 512],
                                              in0=po, in1=x2v(tt)[:, db * 512:(db + 1) * 512]))
                    X.mark(f"op_ev_{u}", "cV")
                    quant(v, 1)
            X.mark("op_v_done", "cV")

        def p5_scalar(s):
            X.wait(s, "op_done")
            for sw in range(4):
                X.wait(s, f"qgrp_0_{sw}")
            for fc in range(8):
                X.dma("w2x", s.dma_start(
                    out=w2qTv(fc),
                    in_=w2q_hbm[:, fc * 128:(fc + 1) * 128],
                    transpose=True))
            X.mark("w2x_0", "w2x")
            for i in range(8):
                X.wait(s, f"op_ev_{24+i}")
                if i >= 1:
                    X.wait(s, f"ln2_var_{i-1}")
                X.chain(s, "cS", s.activation(out=ln2sq, in_=x2v(i), func=AF.Identity,
                                              accum_out=statc[:, LN_S0:LN_S0 + 1]))
                X.bump("cS", s.activation(out=ln2sq, in_=x2v(i), func=AF.Square,
                                          accum_out=statc[:, LN_S2:LN_S2 + 1]))
                X.mark(f"ln2_sums_{i}", "cS")
                X.wait(s, f"ln2_var_{i}")
                X.chain(s, "cS", s.activation(out=statc[:, LN_RSTD:LN_RSTD + 1],
                                              in_=statc[:, LN_VAR:LN_VAR + 1], func=AF.Ln,
                                              bias=cst[:, 1:2]))
                X.chain(s, "cS", s.activation(out=statc[:, LN_RSTD:LN_RSTD + 1],
                                              in_=statc[:, LN_RSTD:LN_RSTD + 1], func=AF.Exp,
                                              scale=-0.5))
                X.mark(f"ln2_rstd_{i}", "cS")
                X.wait(s, f"ln2_nmr_{i}")
                if i >= 1:
                    X.wait(s, f"ln2_sp_{i-1}")
                X.chain(s, "cS", s.activation(out=x2st,
                                              in_=x2v(i), func=AF.Identity,
                                              scale=statc[:, LN_RSTD:LN_RSTD + 1],
                                              bias=statc[:, LN_NMR:LN_NMR + 1]))
                X.mark(f"ln2_norm_{i}", "cS")
                X.dma("l2q", s.dma_start(out=x2ln_hbm[i * 128:(i + 1) * 128, :],
                                        in_=x2st))
                X.mark(f"ln2_sp_{i}", "l2q")
            X.mark("ln2_spills", "l2q")

        def p5_vec(v):
            for i in range(8):
                X.wait(v, f"ln2_sums_{i}")
                X.chain(v, "cV", v.tensor_scalar(out=statc[:, LN_MEAN:LN_MEAN + 1],
                                                 in0=statc[:, LN_S0:LN_S0 + 1], scalar1=INV_D,
                                                 scalar2=None, op0=ALU.mult))
                X.chain(v, "cV", v.tensor_scalar(out=statc[:, LN_E2:LN_E2 + 1],
                                                 in0=statc[:, LN_S2:LN_S2 + 1], scalar1=INV_D,
                                                 scalar2=None, op0=ALU.mult))
                X.chain(v, "cV", v.tensor_mul(out=statc[:, LN_VAR:LN_VAR + 1],
                                              in0=statc[:, LN_MEAN:LN_MEAN + 1],
                                              in1=statc[:, LN_MEAN:LN_MEAN + 1]))
                X.chain(v, "cV", v.tensor_sub(out=statc[:, LN_VAR:LN_VAR + 1],
                                              in0=statc[:, LN_E2:LN_E2 + 1],
                                              in1=statc[:, LN_VAR:LN_VAR + 1]))
                X.mark(f"ln2_var_{i}", "cV")
                X.wait(v, f"ln2_rstd_{i}")
                X.chain(v, "cV", v.tensor_scalar(out=statc[:, LN_NMR:LN_NMR + 1],
                                                 in0=statc[:, LN_MEAN:LN_MEAN + 1],
                                                 scalar1=statc[:, LN_RSTD:LN_RSTD + 1],
                                                 scalar2=-1.0, op0=ALU.mult, op1=ALU.mult))
                X.mark(f"ln2_nmr_{i}", "cV")
                quant(v, 4)

        def p6_scalar(s):
            for g in range(NGRP):
                for fc in range(8):
                    X.wait(s, f"f1_mm_{g}_{fc}")
                    if g > 0 and fc == 0:
                        X.wait(s, f"f2_mm_{g-1}_7")
                    for t2 in range(2):
                        bank = (2 * fc + t2) % 4
                        X.bump("cS", s.activation(
                            out=hgv(fc)[:, t2 * 512:(t2 + 1) * 512],
                            in_=psA[:, bank * 512:(bank + 1) * 512],
                            func=AF.Gelu, bias=b1cols[:, g * 8 + fc:g * 8 + fc + 1]))
                    X.mark(f"f1gelu_{g}_{fc}", "cS")

            for tb in range(8):
                X.wait(s, f"f2ev_{NGRP-1}_{tb}")
                X.dma("yq", s.dma_start(out=y[tb * 128:(tb + 1) * 128, :], in_=x2v(tb)))
            X.mark("y_stores", "yq")

        def p6_sync(q):
            X.wait(q, "ln2_spills")
            X.wait(q, "op_done")
            X.wait(q, "w1x_0")
            X.wait(q, "w2x_0")
            for dc in range(16):
                X.dma("l2qb", q.dma_start(out=x2lnfm(dc),
                                         in_=x2ln_hbm[:, dc * 128:(dc + 1) * 128],
                                         transpose=True))
            X.mark("x2ln_fm", "l2qb")
            for g in range(NGRP):
                if g >= 1:
                    X.wait(q, f"f2_mm_{g-1}_7")
                    for sw in range(4):
                        X.wait(q, f"qgrp_{g}_{sw}")
                    for fc in range(8):
                        X.dma("w2x", q.dma_start(
                            out=w2qTv(fc),
                            in_=w2q_hbm[:, g * GF + fc * 128:g * GF + (fc + 1) * 128],
                            transpose=True))
                    X.mark(f"w2x_{g}", "w2x")
                if g + 1 < NGRP:
                    X.wait(q, f"f1_mm_{g}_7")
                    for sw in range(4):
                        X.wait(q, f"qgrp_{g+1}_{sw}")
                    for dc in range(16):
                        X.dma("w1xb", q.dma_start(
                            out=w1qTv(dc),
                            in_=w1q_hbm[(g + 1) * GF:(g + 2) * GF, dc * 128:(dc + 1) * 128],
                            transpose=True))
                    X.mark(f"w1x_{g+1}", "w1xb")

        def p6_tensor(t):
            X.wait(t, "x2ln_fm")
            for g in range(NGRP):
                X.wait(t, f"w1x_{g}")
                for fc in range(8):
                    if g > 0 or fc >= 2:
                        pg, pf = (g, fc - 2) if fc >= 2 else (g - 1, fc + 6)
                        X.wait(t, f"f1gelu_{pg}_{pf}")
                    if g > 0 and fc < 2:
                        X.wait(t, f"f2ev_{g-1}_7")
                    inst = None
                    for dc in range(16):
                        lhs = w1qTv(dc)[:, fc * 128:(fc + 1) * 128]
                        for t2 in range(2):
                            bank = (2 * fc + t2) % 4
                            inst = t.matmul(psA[:, bank * 512:(bank + 1) * 512], lhs,
                                            x2lnfm(dc)[:, t2 * 512:(t2 + 1) * 512],
                                            start=(dc == 0), stop=(dc == 15))
                    X.bump("cT", inst)
                    X.mark(f"f1_mm_{g}_{fc}", "cT")
                X.wait(t, f"w2x_{g}")
                X.wait(t, f"f1gelu_{g}_7")
                for tb in range(8):
                    ps = psB if tb % 2 == 0 else psA
                    if g > 0 or tb >= 2:
                        pg, pt = (g, tb - 2) if tb >= 2 else (g - 1, tb + 6)
                        X.wait(t, f"f2ev_{pg}_{pt}")
                    inst = None
                    for fc in range(8):
                        lhs = hgv(fc)[:, tb * 128:(tb + 1) * 128]
                        for db in range(4):
                            inst = t.matmul(ps[:, db * 512:(db + 1) * 512], lhs,
                                            w2qTv(fc)[:, db * 512:(db + 1) * 512],
                                            start=(fc == 0),
                                            stop=(fc == 7 and g > 0))
                    if g == 0:
                        for db in range(4):
                            inst = t.matmul(ps[:, db * 512:(db + 1) * 512], onesb[0:1, :],
                                            brows[0:1, 4096 + db * 512:4608 + db * 512],
                                            start=False, stop=True)
                    X.bump("cT", inst)
                    X.mark(f"f2_mm_{g}_{tb}", "cT")
            X.mark("ffn_done", "cT")

        def p6_vec(v):
            for g in range(NGRP):
                for tb in range(8):
                    ps = psB if tb % 2 == 0 else psA
                    X.wait(v, f"f2_mm_{g}_{tb}")
                    for db in range(4):
                        X.bump("cV", v.tensor_add(out=x2v(tb)[:, db * 512:(db + 1) * 512],
                                                  in0=ps[:, db * 512:(db + 1) * 512],
                                                  in1=x2v(tb)[:, db * 512:(db + 1) * 512]))
                    X.mark(f"f2ev_{g}_{tb}", "cV")
            quant(v, 2 * NQU)
            X.mark("ffn_v_done", "cV")

        def p7_sync(q):
            X.wait(q, "y_stores")
            X.wait(q, "ffn_done")
            X.wait(q, "ffn_v_done")
            for b_ in range(3):
                X.wait(q, f"lqfin_{b_}")
            for sw in range(4):
                X.wait(q, f"wsfin_{sw}")

        PHASES = [
            (p0_sync, "sync"), (p0_pool, "gpsimd"), (p0_vec, "vector"),
            (p1_sync, "sync"), (p1_scalar, "scalar"), (p1_vec, "vector"),
            (p2_sync, "sync"), (p2_tensor, "tensor"),
            (p2_vec, "vector"), (p2_vec_b, "vector"), (p2_scalar, "scalar"),
            (p2c_sync, "sync"), (p2c_scalar, "scalar"), (p2c_tensor, "tensor"), (p2c_vec, "vector"),
            (p3_sync, "sync"), (p3_pool, "gpsimd"), (p3_tensor, "tensor"),
            (p3_scalar, "scalar"), (p3_vec, "vector"),
            (p4_scalar, "scalar"), (p4_sync, "sync"), (p4_pool, "gpsimd"),
            (p4_tensor, "tensor"), (p4_vec, "vector"),
            (p5_scalar, "scalar"), (p5_vec, "vector"),
            (p6_scalar, "scalar"), (p6_sync, "sync"),
            (p6_tensor, "tensor"), (p6_vec, "vector"),
            (p7_sync, "sync"),
        ]

        for fn, _ in PHASES:
            fn(_DryEng())
        X.reset()
        QS["ld"] = QS["sp"] = QS["q"] = QS["g"] = 0
        block = stk.enter_context(nc.Block())
        for fn, engname in PHASES:
            dec = getattr(block, engname)

            def _emit(eng, _fn=fn):
                _fn(eng)
            dec(_emit)

    return nc


_NC_CACHE = {}


def _perm(half):
    own = PAIRS_A if half == 0 else PAIRS_B
    par = PAIRS_B if half == 0 else PAIRS_A
    rows = []
    for ga, gb in own + par:
        rows.extend(range(ga * 128, ga * 128 + 128))
        rows.extend(range(gb * 128, gb * 128 + 128))
    return np.array(rows, dtype=np.int64)


def _build_masks(half):
    own = PAIRS_A if half == 0 else PAIRS_B
    par = PAIRS_B if half == 0 else PAIRS_A
    m = np.zeros((128, 4096), dtype=np.float32)
    p = np.arange(128)[:, None]
    qq = np.arange(128)[None, :]
    for j in range(NPAIR):
        for sidx in range(4):
            gk = own[j][sidx] if sidx < 2 else par[j][sidx - 2]
            kpos = gk * 128 + p
            base = (j * 4 + sidx) * 256
            m[:, base:base + 128] = (kpos <= own[j][0] * 128 + qq)
            m[:, base + 128:base + 256] = (kpos <= own[j][1] * 128 + qq)
    return m.astype(BF)


def kernel(L0_attn_x, L0_attn_w_qkv, L0_attn_b_qkv, L0_attn_w_out, L0_attn_b_out,
           L0_ffn_w1, L0_ffn_b1, L0_ffn_w2, L0_ffn_b2):
    x = np.ascontiguousarray(np.asarray(L0_attn_x, dtype=np.float32))
    wqkv = np.ascontiguousarray(np.asarray(L0_attn_w_qkv, dtype=np.float32))
    bqkv = np.ascontiguousarray(np.asarray(L0_attn_b_qkv, dtype=np.float32))
    wout = np.ascontiguousarray(np.asarray(L0_attn_w_out, dtype=np.float32))
    bout = np.ascontiguousarray(np.asarray(L0_attn_b_out, dtype=np.float32))
    w1 = np.ascontiguousarray(np.asarray(L0_ffn_w1, dtype=np.float32))
    b1 = np.ascontiguousarray(np.asarray(L0_ffn_b1, dtype=np.float32))
    w2 = np.ascontiguousarray(np.asarray(L0_ffn_w2, dtype=np.float32))
    b2 = np.ascontiguousarray(np.asarray(L0_ffn_b2, dtype=np.float32))
    B = x.shape[0]

    if "nc" not in _NC_CACHE:
        _NC_CACHE["nc"] = build_program()
    nc = _NC_CACHE["nc"]

    in_maps = []
    meta = []
    for c in range(8):
        b, half = c // 2, c % 2
        perm = _perm(half)
        meta.append((b, perm))
        in_maps.append({
            "xg": np.ascontiguousarray(x[b][perm]),
            "masks": _build_masks(half),
            "wqkv": wqkv, "bqkv": bqkv, "wout": wout, "bout": bout,
            "w1": w1, "b1": b1, "w2": w2, "b2": b2,
        })

    res = run_bass_kernel_spmd(nc, in_maps, core_ids=list(range(8)))
    out = np.empty((B, T, D), dtype=np.float32)
    for c in range(8):
        b, perm = meta[c]
        out[b][perm[:TO]] = res.results[c]["y"]
    return out
